# version 03
# baseline (speedup 1.0000x reference)
import numpy as np
import ml_dtypes
from contextlib import ExitStack

import concourse.bass as bass
import concourse.mybir as mybir
from concourse.bass_utils import run_bass_kernel_spmd

F32 = mybir.dt.float32
BF16 = mybir.dt.bfloat16
AF = mybir.ActivationFunctionType
ALU = mybir.AluOpType
AX = mybir.AxisListType
BF = ml_dtypes.bfloat16

EPS32 = float(np.finfo(np.float32).eps)
SQ8 = float(np.sqrt(8.0))

PB_N = 53248
PF_N = 8192
OH = 0
OQ = 16384
OKV = 32768
OWA = 40960
OWS = 49152
OLNR = OQ + 15360
PJ = 0
PWK = 4096
PPZ = 5120
PROW0 = 7168
PROW1 = 7680
PL0 = 4096
PL1 = 4112
PGLO = 4128
PMXS = 4140
PGLA = 4160
PGLB = 4288
PGLC = 4416
PMXQ = 4456
PMXK = 4500
PRC = 4544
PTMP = 4560
PCEPS1 = 4592
PCEPS2 = 4593
PUW = 4608
BNW, BBQ, BBK, BBO, BBFQ, BBFG, BGB = 0, 8, 16, 24, 32, 40, 48
BNEGB, BLI0, BLI1, BLS0, BLS1, BDEN = 56, 57, 58, 59, 60, 61

CCG = [[0, 1], [2, 3], [4, 5], [6, 7]]
ENGINES = ("sync", "scalar", "vector", "tensor", "gpsimd")


def n_live(qt):
    return min(8, (qt * 128 + 2048 + 132 + 511) // 512)


class Plan:
    def __init__(self):
        self.steps = {e: [] for e in ENGINES}
        self.cnt = {s: 0 for s in ("WL", "IO", "PE", "VE", "AC", "PO", "CC")}
        self.sems = {}
        self.log = []

    def val(self, name):
        return self.cnt[name]

    def step(self, eng, emitfn, w=(), i=()):
        for s, v in w:
            assert v <= self.cnt[s], f"[{eng}] wait {s}>={v} but only {self.cnt[s]} planned"
        last = None
        for s, a in i:
            self.cnt[s] += a
            last = self.cnt[s]
        self.log.append((eng, tuple(w), tuple((s, self.cnt[s]) for s, a in i)))
        sems = self.sems
        w = tuple(w)
        i = tuple(i)

        def run(E, w=w, i=i, emitfn=emitfn):
            for s, v in w:
                E.wait_ge(sems[s], v)
            inst = None
            if emitfn is not None:
                inst = emitfn(E)
            for s, a in i:
                assert inst is not None, "inc without instruction"
                inst = inst.then_inc(sems[s], a)
        self.steps[eng].append(run)
        return last


def _bcol(v):
    return np.ascontiguousarray(np.asarray(v, np.float32).reshape(8, 128))


def make_core_inputs(inputs):
    f = np.float32
    sh = {}
    sh["wqkvT"] = np.ascontiguousarray(np.asarray(inputs["L0_attn_a_w_qkv"], f).T)
    sh["woutT"] = np.ascontiguousarray(np.asarray(inputs["L0_attn_a_w_out"], f).T)
    sh["nw8"] = _bcol(inputs["L0_attn_a_norm_weight"])
    pm = np.asarray(inputs["L0_attn_a_persist_mem"], f)
    pmk = np.zeros((128, 32), f)
    pmv = np.zeros((4, 1024), f)
    for hpi in range(8):
        for hd in range(2):
            h_ = 2 * hpi + hd
            pmk[hd * 64:(hd + 1) * 64, hpi * 4:hpi * 4 + 4] = pm[h_].T
            pmv[:, hpi * 128 + hd * 64: hpi * 128 + (hd + 1) * 64] = pm[h_]
    sh["pmk"] = pmk.astype(BF)
    sh["pmv"] = pmv.astype(BF)
    for k in ("wq", "wk", "wv", "wo", "wfq", "wfg"):
        sh[k] = np.ascontiguousarray(np.asarray(inputs[f"L0_attn_b_{k}"], f))
    for k in ("bq", "bk", "bo", "bfq", "bfg"):
        sh[k + "8"] = _bcol(inputs[f"L0_attn_b_{k}"])
    sh["bv_row"] = np.asarray(inputs["L0_attn_b_bv"], f).reshape(1, 1024).astype(BF)
    up = np.asarray(inputs["L0_attn_b_up"], f)
    uz = np.asarray(inputs["L0_attn_b_uz"], f)
    U = np.zeros((8, 128, 32), f)
    for c in range(8):
        for r in range(2):
            h_ = 2 * c + r
            U[c, r * 64:(r + 1) * 64, h_] = up[h_, 0]
            U[c, r * 64:(r + 1) * 64, 16 + h_] = uz[h_, 0]
    sh["U"] = np.ascontiguousarray(U.reshape(1024, 32)).astype(BF)
    sh["sgT"] = np.ascontiguousarray(np.asarray(inputs["L0_ffn_sg"], f).T)
    sh["suT"] = np.ascontiguousarray(np.asarray(inputs["L0_ffn_su"], f).T)
    sh["sdT"] = np.ascontiguousarray(np.asarray(inputs["L0_ffn_sd"], f).T)
    sh["rgT"] = np.ascontiguousarray(np.transpose(np.asarray(inputs["L0_ffn_rg"], f), (0, 2, 1)))
    sh["ruT"] = np.ascontiguousarray(np.transpose(np.asarray(inputs["L0_ffn_ru"], f), (0, 2, 1)))
    sh["rdT"] = np.ascontiguousarray(np.transpose(np.asarray(inputs["L0_ffn_rd"], f), (0, 2, 1)))
    sh["wrT"] = np.ascontiguousarray(np.asarray(inputs["L0_ffn_wr"], f).T)
    sh["rb_row"] = np.asarray(inputs["L0_ffn_rb"], f).reshape(1, 7).astype(BF)

    sh["identb"] = np.eye(128).astype(BF)
    sh["identf"] = np.eye(128, dtype=f)
    sh["negidentb"] = (-np.eye(128)).astype(BF)
    perm = np.zeros((128, 128), f)
    for i in range(64):
        perm[2 * i + 1, 2 * i] = 1.0
        perm[2 * i, 2 * i + 1] = 1.0
    sh["permb"] = perm.astype(BF)
    sh["onesb"] = np.ones((128, 128), BF)
    canon = np.zeros((128, 6 * 512), f)
    rr = np.arange(128)[:, None]
    cc = np.arange(512)[None, :]
    for t in range(4):
        canon[:, (1 + t) * 512:(2 + t) * 512] = np.where(cc <= t * 128 + rr, 0.0, -1.0e9)
    canon[:, 5 * 512:] = -1.0e9
    sh["canon"] = canon.astype(BF)
    sh["Jrow"] = np.broadcast_to(np.arange(4096, dtype=f), (128, 4096)).copy()

    cores = []
    x = np.asarray(inputs["L0_attn_a_x"], f)
    inv_freq = 1.0 / (10000.0 ** (np.arange(0, 64, 2, dtype=np.float64) / 64))
    for cid in range(8):
        b, h = cid // 2, cid % 2
        d = dict(sh)
        d["xT"] = np.ascontiguousarray(x[b, h * 2048:(h + 1) * 2048, :].T)
        pos = np.arange(h * 2048, (h + 1) * 2048, dtype=np.float64)
        fr = np.repeat(pos[None, :] * inv_freq[:, None], 2, axis=0)
        cos = np.cos(fr)
        sgn = np.where((np.arange(64) % 2) == 0, -1.0, 1.0)[:, None]
        sin = np.sin(fr) * sgn
        d["cosT"] = np.ascontiguousarray(np.tile(cos, (2, 1)).astype(f))
        d["sinT"] = np.ascontiguousarray(np.tile(sin, (2, 1)).astype(f))
        canon6 = canon.astype(BF).reshape(128, 6, 512).transpose(1, 0, 2)
        th = 4 * h
        sel = np.zeros((16, 8), np.int64)
        for qt in range(16):
            t_ = th + qt // 4
            for ci in range(8):
                sel[qt, ci] = 0 if ci < t_ else ((1 + qt % 4) if ci == t_ else 5)
        d["masksel"] = np.ascontiguousarray(
            canon6[sel.reshape(-1)].reshape(16 * 8 * 128, 512))
        cores.append(d)
    return cores


def build_nc(debug_taps=(), nphases=8, core_lim=(8, 16), core_stage=4):
    nc = bass.Bass("TRN2", target_bir_lowering=False)
    di = {}

    def inp(name, shape, dt=F32):
        di[name] = nc.dram_tensor(name, list(shape), dt, kind="ExternalInput")
        return di[name]

    inp("xT", (1024, 2048))
    inp("wqkvT", (1024, 3072)); inp("woutT", (1024, 1024)); inp("nw8", (8, 128))
    inp("pmk", (128, 32), BF16); inp("pmv", (4, 1024), BF16)
    for k in ("wq", "wk", "wv", "wo", "wfq", "wfg"):
        inp(k, (1024, 1024))
    for k in ("bq8", "bk8", "bo8", "bfq8", "bfg8"):
        inp(k, (8, 128))
    inp("bv_row", (1, 1024), BF16); inp("U", (1024, 32), BF16)
    inp("sgT", (1024, 2816)); inp("suT", (1024, 2816)); inp("sdT", (2816, 1024))
    inp("rgT", (7, 1024, 2816)); inp("ruT", (7, 1024, 2816)); inp("rdT", (7, 2816, 1024))
    inp("wrT", (1024, 7)); inp("rb_row", (1, 7), BF16)
    inp("identb", (128, 128), BF16); inp("identf", (128, 128)); inp("negidentb", (128, 128), BF16)
    inp("permb", (128, 128), BF16); inp("onesb", (128, 128), BF16)
    inp("canon", (128, 6 * 512), BF16); inp("Jrow", (128, 4096))
    inp("masksel", (16384, 512), BF16)
    inp("cosT", (128, 2048)); inp("sinT", (128, 2048))

    y = nc.dram_tensor("y", [1024, 2048], F32, kind="ExternalOutput")
    cc_k_in = nc.dram_tensor("cc_k_in", [1026, 2048], BF16)
    cc_k_out = nc.dram_tensor("cc_k_out", [2052, 2048], BF16)
    ccvin = nc.dram_tensor("ccvin", [2048, 1024], BF16)
    ccvout = nc.dram_tensor("ccvout", [4096, 1024], BF16)

    taps = {}
    for (tname, pool, p0, np_, c0, ncols) in debug_taps:
        taps[tname] = nc.dram_tensor("tap_" + tname, [np_, ncols], F32, kind="ExternalOutput")

    P = Plan()
    REGS = {}
    with ExitStack() as ST:
        E_ = ST.enter_context
        PBt = E_(nc.sbuf_tensor("PB", [128, PB_N], BF16))
        PFt = E_(nc.sbuf_tensor("PF", [128, PF_N], F32))
        XR = E_(nc.sbuf_tensor("XR", [128, 16384], F32))
        CONSTB = E_(nc.sbuf_tensor("CONSTB", [128, 512], BF16))
        IDF = E_(nc.sbuf_tensor("IDF", [128, 128], F32))
        PMKt = E_(nc.sbuf_tensor("PMK", [128, 32], BF16))
        PMVt = E_(nc.sbuf_tensor("PMV", [4, 1024], BF16))
        UBt = E_(nc.sbuf_tensor("UB", [128, 256], BF16))
        BANKS = E_(nc.sbuf_tensor("BANKS", [128, 64], F32))
        ROWSB = E_(nc.sbuf_tensor("ROWSB", [1, 1036], BF16))
        V8 = E_(nc.sbuf_tensor("V8", [8, 128], F32))
        PS = [E_(nc.psum_tensor(f"PS{i}", [128, 512], F32)) for i in range(6)]
        PST = E_(nc.psum_tensor("PST", [128, 1024], BF16))
        PSM = E_(nc.psum_tensor("PSM", [128, 256], F32))

        for s in ("WL", "IO", "PE", "VE", "AC", "PO", "CC"):
            P.sems[s] = E_(nc.semaphore(s))

        IDB = CONSTB[:, 0:128]
        NIDB = CONSTB[:, 128:256]
        PERM = CONSTB[:, 256:384]
        ONES = CONSTB[:, 384:512]

        def pb(c0, n, p0=0, np_=128):
            return PBt[p0:p0 + np_, c0:c0 + n]

        def pf(c0, n, p0=0, np_=128):
            return PFt[p0:p0 + np_, c0:c0 + n]

        def src3(ap):
            return ap.rearrange("(k p) o -> p k o", p=128)

        def dst3(col, nk, ncols):
            return pb(col, nk * ncols).rearrange("p (k o) -> p k o", k=nk)

        def mm(dst, lh, rh, start, stop, w=(), i=()):
            return P.step("tensor", lambda E, a=dst, b=lh, c=rh, s=start, t=stop:
                          E.matmul(a, b, c, start=s, stop=t), w, i)

        def tr(dst, src, ident, w=(), i=()):
            return P.step("tensor", lambda E, a=dst, b=src, c=ident: E.transpose(a, b, c), w, i)

        def trg(dst, src, ident, start, stop, w=(), i=()):
            return P.step("tensor", lambda E, a=dst, b=src, c=ident, s=start, t=stop:
                          E.matmul(a, b, c, start=s, stop=t, is_transpose=True), w, i)

        def vstep(fn, w=(), i=()):
            return P.step("vector", fn, w, i)

        def astep(fn, w=(), i=()):
            return P.step("scalar", fn, w, i)

        def gstep(fn, w=(), i=()):
            return P.step("gpsimd", fn, w, i)

        def dio(out, in_, w=(), **kw):
            return P.step("sync", lambda E, a=out, b=in_, kw=kw: E.dma_start(out=a, in_=b, **kw),
                          w, [("IO", 16)])

        def dwl(out, in_, w=(), **kw):
            return P.step("gpsimd", lambda E, a=out, b=in_, kw=kw: E.dma_start(out=a, in_=b, **kw),
                          w, [("WL", 16)])

        def barrier():
            ws = [(s, P.val(s)) for s in ("PE", "VE", "AC", "PO", "WL", "IO", "CC")]
            for e in ENGINES:
                P.step(e, None, ws)

        dio(XR[:, :].rearrange("p (a t) -> p a t", a=8), di["xT"][:, :].rearrange("(a p) t -> p a t", p=128))
        dio(CONSTB[:, 0:128], di["identb"][:, :])
        dio(CONSTB[:, 128:256], di["negidentb"][:, :])
        dio(CONSTB[:, 256:384], di["permb"][:, :])
        dio(CONSTB[:, 384:512], di["onesb"][:, :])
        dio(IDF[:, :], di["identf"][:, :])
        dio(PMKt[:, :], di["pmk"][:, :])
        dio(PMVt[:, :], di["pmv"][:, :])
        for c in range(8):
            dio(UBt[:, c * 32:(c + 1) * 32], di["U"][c * 128:(c + 1) * 128, :])
        dio(ROWSB[0:1, 0:1024], di["bv_row"][:, :])
        dio(ROWSB[0:1, 1024:1031], di["rb_row"][:, :])
        dio(PFt[:, 0:2048], di["cosT"][:, :])
        c_setup = dio(PFt[:, 2048:4096], di["sinT"][:, :])
        vstep(lambda E: E.memset(pf(PCEPS1, 1, 0, 1), EPS32 * 1e-5))
        vstep(lambda E: E.memset(pf(PCEPS2, 1, 0, 1), 1e-5), i=[("VE", 1)])
        for vn, bc in (("nw8", BNW), ("bq8", BBQ), ("bk8", BBK), ("bo8", BBO),
                       ("bfq8", BBFQ), ("bfg8", BBFG)):
            c_io = dio(V8[:, :], di[vn][:, :], w=[("VE", P.val("VE"))])
            c_pe = tr(PSM[:, 0:8], V8[0:8, 0:128], IDF[0:8, 0:8], w=[("IO", c_io)], i=[("PE", 1)])
            vstep(lambda E, bc=bc: E.tensor_copy(BANKS[:, bc:bc + 8], PSM[:, 0:8]),
                  w=[("PE", c_pe)], i=[("VE", 1)])
        barrier()

        def ln_phase(which):
            xb = [OWS + 0, OWS + 512]
            xq = [OWS + 1024, OWS + 1536]
            for T in range(4):
                t0 = T * 512
                pe_done = []
                for c in range(8):
                    w = [("PE", pe_done[-2])] if len(pe_done) >= 2 else \
                        ([("PE", P.val("PE"))] if (T > 0 and c == 0) else [])
                    vstep(lambda E, c=c, t0=t0, s=xb[c % 2]: E.tensor_copy(
                        pb(s, 512), XR[:, c * 2048 + t0: c * 2048 + t0 + 512]), w=w)
                    c_ve = vstep(lambda E, c=c, t0=t0, s=xq[c % 2]: E.tensor_tensor(
                        pb(s, 512), XR[:, c * 2048 + t0: c * 2048 + t0 + 512],
                        XR[:, c * 2048 + t0: c * 2048 + t0 + 512], op=ALU.mult), i=[("VE", 1)])
                    mm(PS[0][0:1, :], ONES[:, 0:1], pb(xb[c % 2], 512), c == 0, c == 7,
                       w=[("VE", c_ve)])
                    c_pe = mm(PS[1][0:1, :], ONES[:, 0:1], pb(xq[c % 2], 512), c == 0, c == 7,
                              i=[("PE", 1)])
                    pe_done.append(c_pe)
                RM = pf(PROW0, 512, 0, 1)
                RV = pf(PROW1, 512, 0, 1)
                RT = pf(PWK, 512, 0, 1)
                vstep(lambda E: E.tensor_scalar(RM, PS[0][0:1, :], 1.0 / 1024.0, None, op0=ALU.mult),
                      w=[("PE", pe_done[-1])])
                vstep(lambda E: E.tensor_scalar(RV, PS[1][0:1, :], 1.0 / 1024.0, None, op0=ALU.mult))
                vstep(lambda E: E.tensor_tensor(RT, RM, RM, op=ALU.mult))
                c_ve = vstep(lambda E: E.tensor_tensor(RV, RV, RT, op=ALU.subtract), i=[("VE", 1)])
                lnsc = (1.0 + EPS32) if which == "rms" else 1.0
                lnbc = PCEPS1 if which == "rms" else PCEPS2
                astep(lambda E, s=lnsc, b=lnbc: E.activation(RT, RV, AF.Ln,
                                                             bias=pf(b, 1, 0, 1), scale=s),
                      w=[("VE", c_ve)])
                c_ac = astep(lambda E: E.activation(RV, RT, AF.Exp, scale=-0.5), i=[("AC", 1)])
                c_ve = vstep(lambda E: E.scalar_tensor_tensor(RT, RM, -1.0, RV,
                                                              op0=ALU.mult, op1=ALU.mult),
                             w=[("AC", c_ac)], i=[("VE", 1)])
                R1B = pb(OLNR, 512, 0, 1)
                R0B = pb(OLNR + 512, 512, 0, 1)
                vstep(lambda E: E.tensor_copy(R1B, RV), w=[("VE", c_ve)])
                c_ve = vstep(lambda E: E.tensor_copy(R0B, RT), i=[("VE", 1)])
                mm(PS[4][:, :], ONES[0:1, :], R1B, True, True,
                   w=[("VE", c_ve), ("VE", P.val("VE"))])
                c_pe = mm(PS[5][:, :], ONES[0:1, :], R0B, True, True, i=[("PE", 1)])
                TT = pf(PWK, 512)
                UT = pf(PWK + 512, 512)
                for c in range(8):
                    w = [("PE", c_pe)] if c == 0 else []
                    vstep(lambda E, c=c, t0=t0: E.tensor_tensor(
                        TT, XR[:, c * 2048 + t0: c * 2048 + t0 + 512], PS[4][:, :], op=ALU.mult), w=w)
                    if which == "rms":
                        vstep(lambda E: E.tensor_tensor(UT, TT, PS[5][:, :], op=ALU.add))
                        vstep(lambda E, c=c, t0=t0: E.tensor_scalar(
                            pb(OH + c * 2048 + t0, 512), UT, BANKS[:, BNW + c:BNW + c + 1],
                            None, op0=ALU.mult), i=[("VE", 1)])
                    else:
                        vstep(lambda E, c=c, t0=t0: E.tensor_tensor(
                            pb(OH + c * 2048 + t0, 512), TT, PS[5][:, :], op=ALU.add), i=[("VE", 1)])

        def proj_fm(wsrc, wcol0, nco, post, rhs_off=OH, slotA=OWA, slotB=OWA + 1024):
            slots = [slotA, slotB]
            slot_free = [(), ()]
            ps_free = {0: 0, 1: 0}
            last_ve = 0
            for o in range(nco):
                sl = slots[o % 2]
                c_wl = dwl(dst3(sl, 8, 128),
                           src3(wsrc[:, wcol0 + o * 128: wcol0 + (o + 1) * 128]),
                           w=slot_free[o % 2])
                pes = []
                for T in range(4):
                    psS = PS[T % 2]
                    w0 = [("WL", c_wl), ("VE", ps_free[T % 2])]
                    for kc in range(8):
                        c_pe = mm(psS[:, :], pb(sl + kc * 128, 128),
                                  pb(rhs_off + kc * 2048 + T * 512, 512),
                                  kc == 0, kc == 7,
                                  w=w0 if kc == 0 else (),
                                  i=[("PE", 1)] if kc == 7 else ())
                    pes.append(c_pe)
                    c_psfree, last_ve = post(o, T, psS, c_pe)
                    ps_free[T % 2] = c_psfree
                slot_free[o % 2] = [("PE", pes[-1])]
            return last_ve

        def post_plain(dst_off, bias_bank=None):
            def post(o, T, psS, c_pe):
                if bias_bank is not None:
                    c_ve = vstep(lambda E, o=o, T=T, psS=psS: E.tensor_scalar(
                        pb(dst_off + o * 2048 + T * 512, 512), psS[:, :],
                        BANKS[:, bias_bank + o:bias_bank + o + 1], None, op0=ALU.add),
                        w=[("PE", c_pe)], i=[("VE", 1)])
                else:
                    c_ve = vstep(lambda E, o=o, T=T, psS=psS: E.tensor_copy(
                        pb(dst_off + o * 2048 + T * 512, 512), psS[:, :]),
                        w=[("PE", c_pe)], i=[("VE", 1)])
                return c_ve, c_ve
            return post

        def post_accum_x(bias_bank=None):
            def post(o, T, psS, c_pe):
                xsl = XR[:, o * 2048 + T * 512: o * 2048 + (T + 1) * 512]
                if bias_bank is not None:
                    c_ve = vstep(lambda E, o=o, psS=psS, xsl=xsl: E.scalar_tensor_tensor(
                        xsl, psS[:, :], BANKS[:, bias_bank + o:bias_bank + o + 1], xsl,
                        op0=ALU.add, op1=ALU.add), w=[("PE", c_pe)], i=[("VE", 1)])
                else:
                    c_ve = vstep(lambda E, psS=psS, xsl=xsl: E.tensor_tensor(
                        xsl, xsl, psS[:, :], op=ALU.add), w=[("PE", c_pe)], i=[("VE", 1)])
                return c_ve, c_ve
            return post

        def mk_post_rope(dst_off, stage_rows=None, maxbase=None, maxcells=None):
            st = {"psR": [0, 0], "stg": [0, 0], "msr": [0, 0], "cell": 0}

            def post(o, T, psS, c_pe):
                kraw = OWS + 2048 + (T % 2) * 512
                c_ve1 = vstep(lambda E, kraw=kraw, psS=psS: E.tensor_copy(pb(kraw, 512), psS[:, :]),
                              w=[("PE", c_pe)], i=[("VE", 1)])
                psR = PS[2 + T % 2]
                c_pe2 = mm(psR[:, :], PERM, pb(kraw, 512), True, True,
                           w=[("VE", max(c_ve1, st["psR"][T % 2]))], i=[("PE", 1)])
                TT = pf(PWK, 512)
                UT = pf(PWK + 512, 512)
                c_rot = vstep(lambda E, T=T, psR=psR: E.tensor_tensor(
                    TT, psR[:, :], PFt[:, 2048 + T * 512: 2048 + (T + 1) * 512], op=ALU.mult),
                    w=[("PE", c_pe2)], i=[("VE", 1)])
                st["psR"][T % 2] = c_rot
                vstep(lambda E, kraw=kraw, T=T: E.tensor_tensor(
                    UT, pb(kraw, 512), PFt[:, T * 512:(T + 1) * 512], op=ALU.mult))
                if dst_off is not None:
                    dst = dst_off + o * 2048 + T * 512
                else:
                    dst = OWS + 3072 + (T % 2) * 512
                wext = [("IO", st["stg"][T % 2])] if (stage_rows is not None and dst_off is None) else []
                c_ve = vstep(lambda E, dst=dst: E.tensor_tensor(pb(dst, 512), UT, TT, op=ALU.add),
                             w=wext, i=[("VE", 1)])
                if stage_rows is not None:
                    dram, rowbase = stage_rows
                    c_io = dio(dram[rowbase + o * 128: rowbase + (o + 1) * 128,
                                    T * 512:(T + 1) * 512], pb(dst, 512), w=[("VE", c_ve)])
                    st["stg"][T % 2] = c_io
                if maxbase is not None:
                    SQ = pb(OWS + (T % 2) * 512, 512)
                    c_sq = vstep(lambda E, dst=dst, SQ=SQ: E.tensor_tensor(
                        SQ, pb(dst, 512), pb(dst, 512), op=ALU.mult), i=[("VE", 1)])
                    psROW = PS[4 + T % 2]
                    c_pe3 = mm(psROW[0:1, :], ONES[:, 0:1], SQ,
                               True, True, w=[("VE", max(c_sq, st["msr"][T % 2]))], i=[("PE", 1)])
                    cell = st["cell"]
                    st["cell"] += 1
                    c_red = vstep(lambda E, cell=cell, psROW=psROW: E.tensor_reduce(
                        pf(maxbase + cell, 1, 0, 1), psROW[0:1, :], axis=AX.X, op=ALU.max),
                        w=[("PE", c_pe3)], i=[("VE", 1)])
                    st["msr"][T % 2] = c_red
                    maxcells[0] = max(maxcells[0], st["cell"])
                return c_ve1, c_ve
            return post

        def post_stage_bias(bias_bank, stage_rows, maxbase=None, maxcells=None):
            st = {"stg": [0, 0], "msr": [0, 0], "cell": 0}

            def post(o, T, psS, c_pe):
                dst = OWS + 3072 + (T % 2) * 512
                c_ve = vstep(lambda E, o=o, dst=dst, psS=psS: E.tensor_scalar(
                    pb(dst, 512), psS[:, :], BANKS[:, bias_bank + o: bias_bank + o + 1],
                    None, op0=ALU.add), w=[("PE", c_pe), ("IO", st["stg"][T % 2])], i=[("VE", 1)])
                dram, rowbase = stage_rows
                c_io = dio(dram[rowbase + o * 128: rowbase + (o + 1) * 128,
                                T * 512:(T + 1) * 512], pb(dst, 512), w=[("VE", c_ve)])
                st["stg"][T % 2] = c_io
                if maxbase is not None:
                    SQ = pb(OWS + (T % 2) * 512, 512)
                    c_sq = vstep(lambda E, dst=dst, SQ=SQ: E.tensor_tensor(
                        SQ, pb(dst, 512), pb(dst, 512), op=ALU.mult), i=[("VE", 1)])
                    psROW = PS[4 + T % 2]
                    c_pe3 = mm(psROW[0:1, :], ONES[:, 0:1], SQ, True, True,
                               w=[("VE", max(c_sq, st["msr"][T % 2]))], i=[("PE", 1)])
                    cell = st["cell"]
                    st["cell"] += 1
                    c_red = vstep(lambda E, cell=cell, psROW=psROW: E.tensor_reduce(
                        pf(maxbase + cell, 1, 0, 1), psROW[0:1, :], axis=AX.X, op=ALU.max),
                        w=[("PE", c_pe3)], i=[("VE", 1)])
                    st["msr"][T % 2] = c_red
                    maxcells[0] = max(maxcells[0], st["cell"])
                return c_ve, c_ve
            return post

        def proj_v_nobias(wsrc, wcol0, vslot=OWA + 2048):
            c_prev_pe = P.val("PE")
            c_stage_io = [0, 0]
            for vo in range(2):
                c_wl = dwl(dst3(vslot, 8, 512),
                           src3(wsrc[:, wcol0 + vo * 512: wcol0 + (vo + 1) * 512]),
                           w=[("PE", c_prev_pe)])
                for t16 in range(16):
                    psS = PS[t16 % 2]
                    w0 = [("WL", c_wl), ("VE", P.val("VE") if t16 >= 2 else 0)]
                    for kc in range(8):
                        c_pe = mm(psS[:, :], pb(OH + kc * 2048 + t16 * 128, 128),
                                  pb(vslot + kc * 512, 512), kc == 0, kc == 7,
                                  w=w0 if kc == 0 else (), i=[("PE", 1)] if kc == 7 else ())
                    dst = OWS + 3072 + (t16 % 2) * 512
                    c_ve = vstep(lambda E, dst=dst, psS=psS: E.tensor_copy(pb(dst, 512), psS[:, :]),
                                 w=[("PE", c_pe), ("IO", c_stage_io[t16 % 2])], i=[("VE", 1)])
                    c_io = dio(ccvin[t16 * 128:(t16 + 1) * 128, vo * 512:(vo + 1) * 512],
                               pb(dst, 512), w=[("VE", c_ve)])
                    c_stage_io[t16 % 2] = c_io
                    c_prev_pe = c_pe

        def proj_v_bias(wsrc, wcol0, vslot=OWA + 2048):
            c_prev_pe = P.val("PE")
            c_stage_io = [0, 0]
            for vo in range(2):
                c_wl = dwl(dst3(vslot, 8, 512),
                           src3(wsrc[:, wcol0 + vo * 512: wcol0 + (vo + 1) * 512]),
                           w=[("PE", c_prev_pe)])
                for t16 in range(16):
                    psS = PS[t16 % 2]
                    w0 = [("WL", c_wl), ("VE", P.val("VE") if t16 >= 2 else 0)]
                    for kc in range(8):
                        mm(psS[:, :], pb(OH + kc * 2048 + t16 * 128, 128), pb(vslot + kc * 512, 512),
                           kc == 0, False, w=w0 if kc == 0 else ())
                    c_pe = mm(psS[:, :], ONES[0:1, :], ROWSB[0:1, vo * 512:(vo + 1) * 512],
                              False, True, i=[("PE", 1)])
                    dst = OWS + 3072 + (t16 % 2) * 512
                    c_ve = vstep(lambda E, dst=dst, psS=psS: E.tensor_copy(pb(dst, 512), psS[:, :]),
                                 w=[("PE", c_pe), ("IO", c_stage_io[t16 % 2])], i=[("VE", 1)])
                    c_io = dio(ccvin[t16 * 128:(t16 + 1) * 128, vo * 512:(vo + 1) * 512],
                               pb(dst, 512), w=[("VE", c_ve)])
                    c_stage_io[t16 % 2] = c_io
                    c_prev_pe = c_pe

        def finalize_max(maxbase, ncells, dstcell):
            c = vstep(lambda E, n=ncells: E.tensor_reduce(
                pf(dstcell, 1, 0, 1), pf(maxbase, ncells, 0, 1), axis=AX.X, op=ALU.max),
                i=[("VE", 1)])
            return c

        def make_negB(qcell, kcells, w=()):
            c = vstep(lambda E, k0=kcells[0]: E.tensor_copy(pf(PMXS, 1, 0, 1), pf(k0, 1, 0, 1)),
                      w=w, i=[("VE", 1)])
            for kc_ in kcells[1:]:
                c = vstep(lambda E, kc_=kc_: E.tensor_tensor(
                    pf(PMXS, 1, 0, 1), pf(PMXS, 1, 0, 1), pf(kc_, 1, 0, 1), op=ALU.max),
                    w=[("VE", c)], i=[("VE", 1)])
            c = vstep(lambda E: E.tensor_tensor(pf(PMXS, 1, 0, 1), pf(PMXS, 1, 0, 1),
                                                pf(qcell, 1, 0, 1), op=ALU.add),
                      w=[("VE", c)], i=[("VE", 1)])
            c = vstep(lambda E: E.tensor_scalar(pf(PMXS, 1, 0, 1), pf(PMXS, 1, 0, 1),
                                                -1.0 / 16.0, None, op0=ALU.mult),
                      w=[("VE", c)], i=[("VE", 1)])
            c = vstep(lambda E: E.tensor_copy(ROWSB[0:1, 1031:1032], pf(PMXS, 1, 0, 1)),
                      w=[("VE", c)], i=[("VE", 1)])
            c_pe = mm(PSM[:, 16:17], ONES[0:1, :], ROWSB[0:1, 1031:1032], True, True,
                      w=[("VE", c)], i=[("PE", 1)])
            c = vstep(lambda E: E.tensor_copy(BANKS[:, BNEGB:BNEGB + 1], PSM[:, 16:17]),
                      w=[("PE", c_pe)], i=[("VE", 1)])
            return c

        def attn_core(is_a):
            nhp, nqt = core_lim
            """per hp: load KT/VT from gathered cc, loop qt, chunks; OF accumulated at OH."""
            PTs = [OWS + 1024, OWS + 1536]
            Pslot = [OWS + 0, OWS + 512]
            T2IN = [OWS + 2048, OWS + 2560]
            T2N = [OWS + 3072, OWS + 3584]
            MSK = [OWS + 2176, OWS + 2688]
            msk_io = [0, 0]
            msk_pe = [0, 0]
            c_kv_io = 0
            c_last_avpe = P.val("PE")
            c_of_ve = P.val("VE")
            for hp in range(nhp):
                w_kv = [("CC", P.val("CC")), ("PE", c_last_avpe)]
                dio(pb(OKV, 2048), cc_k_out[hp * 128:(hp + 1) * 128, :], w=w_kv)
                dio(pb(OKV + 2048, 2048), cc_k_out[1026 + hp * 128:1026 + (hp + 1) * 128, :])
                for b8 in range(4):
                    c_kv_io = dio(dst3(OKV + 4096 + b8 * 1024, 8, 128),
                                  src3(ccvout[b8 * 1024:(b8 + 1) * 1024, hp * 128:(hp + 1) * 128]))
                for qt in range(nqt):
                    nl = n_live(qt) if is_a else 8
                    npm = 1 if is_a else 0
                    lbase = PL0 + (qt % 2) * 32
                    if is_a:
                        for cpre in range(min(2, nl)):
                            msk_io[cpre % 2] = dio(
                                pb(MSK[cpre % 2], 512),
                                di["masksel"][(qt * 8 + cpre) * 128:(qt * 8 + cpre + 1) * 128, :],
                                w=[("PE", msk_pe[cpre % 2])])
                    c_exp = {(h_, p_): P.val("AC") for h_ in (0, 1) for p_ in (0, 1)}
                    c_ptev = {0: P.val("VE"), 1: P.val("VE")}
                    av_first = True
                    for ci in range(nl + npm):
                        is_pm = is_a and (ci == nl)
                        for hd in range(2):
                            psS = PS[hd + 2 * (ci % 2)]
                            q_lh = pb(OQ + hp * 2048 + qt * 128, 128, hd * 64, 64)
                            if is_pm:
                                c_pe = mm(PSM[:, hd * 4:hd * 4 + 4], q_lh,
                                          PMKt[hd * 64:(hd + 1) * 64, hp * 4:hp * 4 + 4],
                                          True, True,
                                          w=[("AC", max(c_exp[(0, 0)], c_exp[(0, 1)])),
                                             ("AC", max(c_exp[(1, 0)], c_exp[(1, 1)]))],
                                          i=[("PE", 1)])
                                c_ac = astep(lambda E, hd=hd, ci=ci, lbase=lbase: E.activation(
                                    pb(Pslot[hd], 4), PSM[:, hd * 4:hd * 4 + 4], AF.Exp,
                                    bias=BANKS[:, BNEGB:BNEGB + 1], scale=0.125,
                                    accum_out=pf(lbase + hd * 16 + ci, 1)),
                                    w=[("PE", c_pe)], i=[("AC", 1)])
                            else:
                                wS = [("AC", c_exp[(hd, ci % 2)]), ("IO", c_kv_io)]
                                mm(psS[:, :], q_lh, pb(OKV + ci * 512, 512, hd * 64, 64),
                                   True, False, w=wS)
                                if is_a:
                                    c_pe = mm(psS[:, :], IDB, pb(MSK[ci % 2], 512),
                                              False, True,
                                              w=[("IO", msk_io[ci % 2])], i=[("PE", 1)])
                                    if hd == 1:
                                        msk_pe[ci % 2] = c_pe
                                        if ci + 2 < nl:
                                            msk_io[ci % 2] = dio(
                                                pb(MSK[ci % 2], 512),
                                                di["masksel"][(qt * 8 + ci + 2) * 128:(qt * 8 + ci + 3) * 128, :],
                                                w=[("PE", c_pe)])
                                else:
                                    c_ve = vstep(lambda E, hd=hd, ci=ci, qt=qt: E.tensor_scalar(
                                        pb(T2IN[hd], 512), PFt[:, ci * 512:(ci + 1) * 512],
                                        PFt[:, PUW + qt * 32 + hd:PUW + qt * 32 + hd + 1],
                                        PFt[:, PUW + qt * 32 + 16 + hd:PUW + qt * 32 + 16 + hd + 1],
                                        op0=ALU.mult, op1=ALU.subtract),
                                        w=[("PO", P.val("PO"))], i=[("VE", 1)])
                                    c_po = gstep(lambda E, hd=hd: E.tensor_tensor(
                                        pb(T2N[hd], 512), pb(T2IN[hd], 512), pb(T2IN[hd], 512),
                                        op=ALU.mult), w=[("VE", c_ve)], i=[("PO", 1)])
                                    c_pe = mm(psS[:, :], NIDB, pb(T2N[hd], 512), False, True,
                                              w=[("PO", c_po)], i=[("PE", 1)])
                                c_ac = astep(lambda E, hd=hd, ci=ci, psS=psS, lbase=lbase: E.activation(
                                    pb(Pslot[hd], 512), psS[:, :], AF.Exp,
                                    bias=BANKS[:, BNEGB:BNEGB + 1], scale=0.125,
                                    accum_out=pf(lbase + hd * 16 + ci, 1)),
                                    w=[("PE", c_pe)], i=[("AC", 1)])
                            c_exp[(hd, ci % 2)] = c_ac
                            if core_stage < 2:
                                continue
                            nblk = 1 if is_pm else 4
                            wtr = [("AC", c_ac), ("VE", c_ptev[0]), ("VE", c_ptev[1])]
                            for k in range(nblk):
                                if is_pm:
                                    c_pe = trg(PST[0:4, hd * 512:hd * 512 + 128],
                                               pb(Pslot[hd], 4), IDB, True, True, w=wtr,
                                               i=[("PE", 1)])
                                else:
                                    c_pe = trg(PST[:, hd * 512 + k * 128: hd * 512 + (k + 1) * 128],
                                               pb(Pslot[hd] + k * 128, 128), IDB,
                                               k == 0, k == nblk - 1,
                                               w=wtr if k == 0 else (),
                                               i=[("PE", 1)] if k == nblk - 1 else ())
                            if is_pm:
                                c_ve = vstep(lambda E, hd=hd: E.tensor_copy(
                                    pb(OWS + 3712 + hd * 128, 128, 0, 4),
                                    PST[0:4, hd * 512:hd * 512 + 128]),
                                    w=[("PE", c_pe)], i=[("VE", 1)])
                            else:
                                c_ve = vstep(lambda E, hd=hd: E.tensor_copy(
                                    pb(PTs[hd], 512), PST[:, hd * 512:(hd + 1) * 512]),
                                    w=[("PE", c_pe)], i=[("VE", 1)])
                            c_ptev[hd] = c_ve
                            if core_stage < 3:
                                continue
                            psAV = PS[4 + hd]
                            for k in range(nblk):
                                if is_pm:
                                    c_avpe = mm(psAV[:, 0:64],
                                                pb(OWS + 3712 + hd * 128, 128, 0, 4),
                                                PMVt[0:4, hp * 128 + hd * 64: hp * 128 + (hd + 1) * 64],
                                                False, True, w=[("VE", c_ve)], i=[("PE", 1)])
                                else:
                                    blk = ci * 4 + k
                                    wav = [("VE", c_ve)] if k == 0 else []
                                    if av_first and k == 0:
                                        wav.append(("VE", c_of_ve))
                                    c_avpe = mm(psAV[:, 0:64], pb(PTs[hd] + k * 128, 128),
                                                pb(OKV + 4096 + blk * 128 + hd * 64, 64),
                                                av_first and k == 0,
                                                (not is_a) and ci == 7 and k == 3,
                                                w=wav, i=[("PE", 1)])
                        av_first = False
                    if core_stage < 4:
                        continue
                    c_last_avpe = c_avpe
                    nls = nl + npm
                    c0 = vstep(lambda E, nls=nls, lbase=lbase: E.tensor_reduce(
                        BANKS[:, BLS0:BLS0 + 1], pf(lbase, nls), axis=AX.X, op=ALU.add),
                        w=[("AC", max(c_exp[(0, 0)], c_exp[(0, 1)])),
                           ("AC", max(c_exp[(1, 0)], c_exp[(1, 1)]))], i=[("VE", 1)])
                    c1 = vstep(lambda E, nls=nls, lbase=lbase: E.tensor_reduce(
                        BANKS[:, BLS1:BLS1 + 1], pf(lbase + 16, nls), axis=AX.X, op=ALU.add),
                        i=[("VE", 1)])
                    c0 = vstep(lambda E: E.reciprocal(BANKS[:, BLI0:BLI0 + 1], BANKS[:, BLS0:BLS0 + 1]),
                               w=[("VE", c1)], i=[("VE", 1)])
                    c1 = vstep(lambda E: E.reciprocal(BANKS[:, BLI1:BLI1 + 1], BANKS[:, BLS1:BLS1 + 1]),
                               w=[("VE", c0)], i=[("VE", 1)])
                    OSB = OWS + 2048
                    vstep(lambda E: E.tensor_scalar(pb(OSB, 64), PS[4][:, 0:64],
                                                    BANKS[:, BLI0:BLI0 + 1], None, op0=ALU.mult),
                          w=[("VE", c1), ("PE", c_last_avpe)])
                    c_ve = vstep(lambda E: E.tensor_scalar(pb(OSB + 64, 64), PS[5][:, 0:64],
                                                           BANKS[:, BLI1:BLI1 + 1], None, op0=ALU.mult),
                                 i=[("VE", 1)])
                    c_pe = trg(PST[:, 0:128], pb(OSB, 128), IDB, True, True,
                               w=[("VE", c_ve), ("VE", c_ptev[0]), ("VE", c_ptev[1])], i=[("PE", 1)])
                    c_of_ve = vstep(lambda E, hp=hp, qt=qt: E.tensor_copy(
                        pb(OH + hp * 2048 + qt * 128, 128), PST[:, 0:128]),
                        w=[("PE", c_pe)], i=[("VE", 1)])

        def phase_attn_a(lv=4):
            ln_phase("rms")
            barrier()
            if lv < 2:
                return
            qmax = [0]
            kmax = [0]
            proj_fm(di["wqkvT"], 1024, 8,
                    mk_post_rope(None, (cc_k_in, 0), PMXK, kmax))
            proj_v_nobias(di["wqkvT"], 2048)
            c_ve = finalize_max(PMXK, kmax[0], PMXK + 38)
            dwl(cc_k_in[1025:1026, 0:1], pf(PMXK + 38, 1, 0, 1), w=[("VE", c_ve)])
            c_cc = gstep(lambda E: E.collective_compute(
                "AllGather", ALU.bypass, replica_groups=CCG,
                ins=[cc_k_in.ap()], outs=[cc_k_out.ap()]),
                w=[("IO", P.val("IO")), ("WL", P.val("WL"))], i=[("CC", 1)])
            c_cc = gstep(lambda E: E.collective_compute(
                "AllGather", ALU.bypass, replica_groups=CCG,
                ins=[ccvin.ap()], outs=[ccvout.ap()]), i=[("CC", 1)])
            proj_fm(di["wqkvT"], 0, 8, mk_post_rope(OQ, None, PMXQ, qmax))
            c_ve = finalize_max(PMXQ, qmax[0], PMXQ + 38)
            c_sq = vstep(lambda E: E.tensor_tensor(pb(OWS, 32), PMKt[:, :], PMKt[:, :], op=ALU.mult),
                         i=[("VE", 1)])
            c_pe = mm(PSM[0:1, 32:64], ONES[:, 0:1], pb(OWS, 32), True, True,
                      w=[("VE", c_sq)], i=[("PE", 1)])
            c_ve2 = vstep(lambda E: E.tensor_reduce(pf(PMXK + 39, 1, 0, 1), PSM[0:1, 32:64],
                                                    axis=AX.X, op=ALU.max),
                          w=[("PE", c_pe)], i=[("VE", 1)])
            barrier()
            dwl(pf(PMXK + 36, 1, 0, 1), cc_k_out[1025:1026, 0:1], w=[("CC", P.val("CC"))])
            c_wl = dwl(pf(PMXK + 37, 1, 0, 1), cc_k_out[2051:2052, 0:1])
            make_negB(PMXQ + 38, [PMXK + 36, PMXK + 37, PMXK + 39], w=[("WL", c_wl)])
            barrier()
            if lv < 3:
                return
            attn_core(True)
            barrier()
            if lv < 4:
                return
            proj_fm(di["woutT"], 0, 8, post_accum_x(None))
            barrier()

        def phase_ffn():
            ln_phase("plain")
            barrier()
            WRS = OWS + 3952
            c_wl = dwl(dst3(WRS, 8, 7), src3(di["wrT"][:, :]))
            WSM = OWS + 3840
            for t16 in range(16):
                w0 = [("WL", c_wl)] if t16 == 0 else [("VE", P.val("VE"))]
                for kc in range(8):
                    mm(PSM[:, 0:7], pb(OH + kc * 2048 + t16 * 128, 128),
                       pb(WRS + kc * 7, 7), kc == 0, False, w=w0 if kc == 0 else ())
                c_pe = mm(PSM[:, 0:7], ONES[0:1, :], ROWSB[0:1, 1024:1031], False, True,
                          i=[("PE", 1)])
                EL = pf(PWK, 7)
                M1 = pf(PWK + 8, 1)
                EE = pf(PWK + 16, 7)
                EQ = pf(PWK + 24, 7)
                M2 = pf(PWK + 32, 1)
                GE = pf(PWK + 40, 7)
                DEN = pf(PWK + 48, 1)
                DIV = pf(PWK + 56, 1)
                WW = pf(PWK + 64, 7)
                c = vstep(lambda E: E.tensor_copy(EL, PSM[:, 0:7]), w=[("PE", c_pe)], i=[("VE", 1)])
                c = vstep(lambda E: E.tensor_reduce(M1, EL, axis=AX.X, op=ALU.max),
                          w=[("VE", c)], i=[("VE", 1)])
                c = vstep(lambda E: E.tensor_scalar(pf(PWK + 9, 1), M1, -1.0, None, op0=ALU.mult),
                          w=[("VE", c)], i=[("VE", 1)])
                c_ac = astep(lambda E: E.activation(EE, EL, AF.Exp, bias=pf(PWK + 9, 1), scale=1.0),
                             w=[("VE", c)], i=[("AC", 1)])
                c = vstep(lambda E: E.tensor_reduce(M1, EE, axis=AX.X, op=ALU.max),
                          w=[("AC", c_ac)], i=[("VE", 1)])
                c = vstep(lambda E: E.tensor_scalar(EQ, EE, M1, None, op0=ALU.is_equal),
                          w=[("VE", c)], i=[("VE", 1)])
                c = vstep(lambda E: E.scalar_tensor_tensor(EQ, EQ, M1, EE, op0=ALU.mult,
                                                           op1=ALU.subtract), w=[("VE", c)], i=[("VE", 1)])
                c = vstep(lambda E: E.tensor_reduce(M2, EQ, axis=AX.X, op=ALU.min),
                          w=[("VE", c)], i=[("VE", 1)])
                c = vstep(lambda E: E.tensor_scalar(M2, M2, -1.0, None, op0=ALU.mult),
                          w=[("VE", c)], i=[("VE", 1)])
                c = vstep(lambda E: E.tensor_scalar(GE, EE, M2, None, op0=ALU.is_ge),
                          w=[("VE", c)], i=[("VE", 1)])
                c = vstep(lambda E: E.tensor_scalar(DEN, M1, M2, None, op0=ALU.add),
                          w=[("VE", c)], i=[("VE", 1)])
                c = vstep(lambda E: E.reciprocal(DIV, DEN), w=[("VE", c)], i=[("VE", 1)])
                c = vstep(lambda E: E.scalar_tensor_tensor(WW, EE, DIV, GE, op0=ALU.mult,
                                                           op1=ALU.mult), w=[("VE", c)], i=[("VE", 1)])
                c = vstep(lambda E, t16=t16: E.tensor_copy(pb(WSM + t16 * 7, 7), WW),
                          w=[("VE", c)], i=[("VE", 1)])
            barrier()
            OIN = OQ
            WB = OWS + 2816
            WROW = OWS + 3328
            gs = [OKV, OKV + 2048]
            us = [OKV + 4096, OKV + 6144]
            ds = [OWA, OWA + 2816]
            c_gfree = [(), ()]
            c_ufree = [(), ()]
            c_dfree = [(), ()]
            c_silu = [0, 0]
            c_inn = [0, 0]
            c_dpe = [0, 0]
            c_gupe = [0, 0]
            for e in range(8):
                if e == 0:
                    gsrc, usrc, dsrc = di["sgT"][:, :], di["suT"][:, :], di["sdT"][:, :]
                else:
                    gsrc, usrc, dsrc = di["rgT"][e - 1], di["ruT"][e - 1], di["rdT"][e - 1]
                for T in range(4):
                    c_gwl = {}
                    c_uwl = {}
                    for j in range(2):
                        wslot = (c_gfree[j] if T == 0 and e == 0 else [("PE", c_gupe[j])])
                        c_gwl[j] = dwl(dst3(gs[j], 8, 256),
                                       src3(gsrc[:, j * 256:(j + 1) * 256]), w=wslot)
                        c_uwl[j] = dwl(dst3(us[j], 8, 256),
                                       src3(usrc[:, j * 256:(j + 1) * 256]))
                    c_dwl = {}
                    for o in range(2):
                        c_dwl[o] = dwl(dst3(ds[o], 22, 128),
                                       src3(dsrc[:, o * 128:(o + 1) * 128]),
                                       w=(c_dfree[o] if T == 0 and e == 0 else [("PE", c_dpe[o])]))
                    nexto = [2]
                    if e > 0:
                        c_pes = []
                        for jj in range(4):
                            t16 = T * 4 + jj
                            c_pes.append(trg(PST[0:1, jj * 128:(jj + 1) * 128],
                                             pb(WSM + t16 * 7 + (e - 1), 1), IDB,
                                             jj == 0, jj == 3,
                                             w=[("VE", P.val("VE"))] if jj == 0 else (),
                                             i=[("PE", 1)]))
                        c_ve = vstep(lambda E: E.tensor_copy(pb(WROW, 512, 0, 1), PST[0:1, 0:512]),
                                     w=[("PE", c_pes[-1])], i=[("VE", 1)])
                        c_pe = mm(PS[4][:, :], ONES[0:1, :], pb(WROW, 512, 0, 1), True, True,
                                  w=[("VE", c_ve), ("VE", P.val("VE"))], i=[("PE", 1)])
                        c_wbve = vstep(lambda E: E.tensor_copy(pb(WB, 512), PS[4][:, :]),
                                       w=[("PE", c_pe)], i=[("VE", 1)])
                    for ic in range(22):
                        j, which = divmod(ic, 2)
                        psG = PS[ic % 2]
                        psU = PS[2 + ic % 2]
                        wg = []
                        if which == 0:
                            wg.append(("WL", c_gwl[j]))
                        if c_silu[ic % 2]:
                            wg.append(("AC", c_silu[ic % 2]))
                        for kc in range(8):
                            mm(psG[:, :], pb(gs[j % 2] + kc * 256 + which * 128, 128),
                               pb(OH + kc * 2048 + T * 512, 512), kc == 0, kc == 7,
                               w=wg if kc == 0 else ())
                        wu = []
                        if which == 0:
                            wu.append(("WL", c_uwl[j]))
                        if c_inn[ic % 2]:
                            wu.append(("VE", c_inn[ic % 2]))
                        for kc in range(8):
                            c_pe = mm(psU[:, :], pb(us[j % 2] + kc * 256 + which * 128, 128),
                                      pb(OH + kc * 2048 + T * 512, 512), kc == 0, kc == 7,
                                      w=wu if kc == 0 else (),
                                      i=[("PE", 1)] if kc == 7 else ())
                        c_ac = astep(lambda E, ic=ic, psG=psG: E.activation(
                            pb(OIN + ic * 512, 512), psG[:, :], AF.Silu),
                            w=[("PE", c_pe)], i=[("AC", 1)])
                        c_silu[ic % 2] = c_ac
                        c_ve = vstep(lambda E, ic=ic, psU=psU: E.tensor_tensor(
                            pb(OIN + ic * 512, 512), pb(OIN + ic * 512, 512), psU[:, :],
                            op=ALU.mult), w=[("AC", c_ac)], i=[("VE", 1)])
                        c_inn[ic % 2] = c_ve
                        if which == 1:
                            c_gupe[j % 2] = c_pe
                            if j + 2 < 11:
                                jn = j + 2
                                c_gwl[jn] = dwl(dst3(gs[jn % 2], 8, 256),
                                                src3(gsrc[:, jn * 256:(jn + 1) * 256]),
                                                w=[("PE", c_pe)])
                                c_uwl[jn] = dwl(dst3(us[jn % 2], 8, 256),
                                                src3(usrc[:, jn * 256:(jn + 1) * 256]))
                    c_dve = [P.val("VE")] * 2
                    for o in range(8):
                        psD = PS[4 + o % 2]
                        wd = [("WL", c_dwl[o]), ("VE", c_inn[0]), ("VE", c_inn[1]),
                              ("VE", c_dve[o % 2])]
                        if e > 0:
                            wd.append(("VE", c_wbve))
                        for ic in range(22):
                            c_pe = mm(psD[:, :], pb(ds[o % 2] + ic * 128, 128),
                                      pb(OIN + ic * 512, 512), ic == 0, ic == 21,
                                      w=wd if ic == 0 else (),
                                      i=[("PE", 1)] if ic == 21 else ())
                        xsl = XR[:, o * 2048 + T * 512: o * 2048 + (T + 1) * 512]
                        if e == 0:
                            c_ve = vstep(lambda E, psD=psD, xsl=xsl: E.tensor_tensor(
                                xsl, xsl, psD[:, :], op=ALU.add), w=[("PE", c_pe)], i=[("VE", 1)])
                        else:
                            TMP = pf(PWK + 512, 512)
                            vstep(lambda E, psD=psD: E.tensor_tensor(
                                TMP, psD[:, :], pb(WB, 512), op=ALU.mult), w=[("PE", c_pe)])
                            c_ve = vstep(lambda E, xsl=xsl: E.tensor_tensor(
                                xsl, xsl, TMP, op=ALU.add), i=[("VE", 1)])
                        c_dve[o % 2] = c_ve
                        c_dpe[o % 2] = c_pe
                        if nexto[0] < 8:
                            on = nexto[0]
                            nexto[0] += 1
                            c_dwl[on] = dwl(dst3(ds[on % 2], 22, 128),
                                            src3(dsrc[:, on * 128:(on + 1) * 128]),
                                            w=[("PE", c_pe)])
            barrier()

        def phase_attn_b():
            ln_phase("plain")
            barrier()
            qmax = [0]
            kmax = [0]
            proj_fm(di["wq"], 0, 8, post_plain(OQ, BBQ))
            st_ms = [0, 0]
            cellc = [0]
            for o in range(8):
                for T in range(4):
                    SQ = pb(OWS + (T % 2) * 512, 512)
                    c_sq = vstep(lambda E, o=o, T=T, SQ=SQ: E.tensor_tensor(
                        SQ, pb(OQ + o * 2048 + T * 512, 512), pb(OQ + o * 2048 + T * 512, 512),
                        op=ALU.mult), i=[("VE", 1)])
                    psROW = PS[4 + T % 2]
                    c_pe3 = mm(psROW[0:1, :], ONES[:, 0:1], SQ, True, True,
                               w=[("VE", c_sq), ("VE", st_ms[T % 2])], i=[("PE", 1)])
                    cell = cellc[0]
                    cellc[0] += 1
                    c_red = vstep(lambda E, cell=cell, psROW=psROW: E.tensor_reduce(
                        pf(PMXQ + cell, 1, 0, 1), psROW[0:1, :], axis=AX.X, op=ALU.max),
                        w=[("PE", c_pe3)], i=[("VE", 1)])
                    st_ms[T % 2] = c_red
            finalize_max(PMXQ, cellc[0], PMXQ + 38)
            c_glo = 0
            for c in range(8):
                c_glo = vstep(lambda E, c=c: E.tensor_reduce(
                    pf(PGLO + c, 1), pb(OQ + c * 2048, 2048), axis=AX.X, op=ALU.add),
                    i=[("VE", 1)])
            c_pe = tr(PSM[0:8, 128:256], pf(PGLO, 8), IDF[:, :], w=[("VE", c_glo)], i=[("PE", 1)])
            c_ve = vstep(lambda E: E.tensor_copy(V8[0:8, :], PSM[0:8, 128:256]),
                         w=[("PE", c_pe)], i=[("VE", 1)])
            dwl(cc_k_in[1024:1025, 0:1024].rearrange("a (b c) -> (a b) c", b=8), V8[0:8, :],
                w=[("VE", c_ve)])
            proj_fm(di["wk"], 0, 8, post_stage_bias(BBK, (cc_k_in, 0), PMXK, kmax))
            proj_v_bias(di["wv"], 0)
            c_ve = finalize_max(PMXK, kmax[0], PMXK + 38)
            dwl(cc_k_in[1025:1026, 0:1], pf(PMXK + 38, 1, 0, 1), w=[("VE", c_ve)])
            gstep(lambda E: E.collective_compute(
                "AllGather", ALU.bypass, replica_groups=CCG,
                ins=[cc_k_in.ap()], outs=[cc_k_out.ap()]),
                w=[("IO", P.val("IO")), ("WL", P.val("WL"))], i=[("CC", 1)])
            gstep(lambda E: E.collective_compute(
                "AllGather", ALU.bypass, replica_groups=CCG,
                ins=[ccvin.ap()], outs=[ccvout.ap()]), i=[("CC", 1)])
            proj_fm(di["wfq"], 0, 8, post_plain(OH), rhs_off=OQ)
            barrier()
            dwl(pf(PGLA, 128, 0, 8), cc_k_out[1024:1025, 0:1024].rearrange("a (b c) -> (a b) c", b=8),
                w=[("CC", P.val("CC"))])
            c_wl = dwl(pf(PGLB, 128, 0, 8), cc_k_out[2050:2051, 0:1024].rearrange("a (b c) -> (a b) c", b=8))
            c = vstep(lambda E: E.tensor_tensor(pf(PGLA, 128, 0, 8), pf(PGLA, 128, 0, 8),
                                                pf(PGLB, 128, 0, 8), op=ALU.add),
                      w=[("WL", c_wl)], i=[("VE", 1)])
            c = vstep(lambda E: E.tensor_scalar(pf(PGLA, 128, 0, 8), pf(PGLA, 128, 0, 8),
                                                1.0 / 4096.0, None, op0=ALU.mult),
                      w=[("VE", c)], i=[("VE", 1)])
            c_pe = tr(PSM[:, 32:40], pf(PGLA, 128, 0, 8), IDF[0:8, 0:8], w=[("VE", c)], i=[("PE", 1)])
            c = vstep(lambda E: E.tensor_copy(pf(PGLC, 8), PSM[:, 32:40]),
                      w=[("PE", c_pe)], i=[("VE", 1)])
            c = vstep(lambda E: E.tensor_copy(pb(OWS + 3072, 8), pf(PGLC, 8)),
                      w=[("VE", c)], i=[("VE", 1)])
            dwl(pf(PMXK + 36, 1, 0, 1), cc_k_out[1025:1026, 0:1])
            c_wl = dwl(pf(PMXK + 37, 1, 0, 1), cc_k_out[2051:2052, 0:1])
            make_negB(PMXQ + 38, [PMXK + 36, PMXK + 37], w=[("WL", c_wl), ("VE", c)])
            slots = [OWA, OWA + 1024]
            slot_free = [(), ()]
            for o in range(8):
                c_wl = dwl(dst3(slots[o % 2], 8, 128),
                           src3(di["wfg"][:, o * 128:(o + 1) * 128]),
                           w=slot_free[o % 2])
                for kc in range(8):
                    c_pe = mm(PSM[:, 40:41], pb(slots[o % 2] + kc * 128, 128),
                              pb(OWS + 3072 + kc, 1), kc == 0, kc == 7,
                              w=[("WL", c_wl), ("VE", P.val("VE"))] if kc == 0 else (),
                              i=[("PE", 1)] if kc == 7 else ())
                c_ve = vstep(lambda E, o=o: E.tensor_scalar(
                    BANKS[:, BGB + o:BGB + o + 1], PSM[:, 40:41],
                    BANKS[:, BBFQ + o:BBFQ + o + 1], BANKS[:, BBFG + o:BBFG + o + 1],
                    op0=ALU.add, op1=ALU.add), w=[("PE", c_pe)], i=[("VE", 1)])
                slot_free[o % 2] = [("PE", c_pe)]
            barrier()
            CS = [OWS + 1024, OWS + 1536]
            for T in range(4):
                for o in range(8):
                    c_ac = astep(lambda E, o=o, T=T, s=CS[o % 2]: E.activation(
                        pb(s, 512), pb(OH + o * 2048 + T * 512, 512), AF.Tanh,
                        bias=BANKS[:, BGB + o:BGB + o + 1]),
                        w=[("PE", P.val("PE"))] if o >= 2 else (), i=[("AC", 1)])
                    c_pe = mm(PS[4][0:32, :], UBt[:, o * 32:(o + 1) * 32], pb(CS[o % 2], 512),
                              o == 0, o == 7, w=[("AC", c_ac)], i=[("PE", 1)])
                c_ac = astep(lambda E, T=T: E.activation(
                    pf(PPZ + T * 512, 512, 0, 32), PS[4][0:32, :], AF.Sigmoid),
                    w=[("PE", c_pe)], i=[("AC", 1)])
                vstep(lambda E, T=T: E.tensor_scalar(
                    pf(PPZ + T * 512, 512, 0, 32), pf(PPZ + T * 512, 512, 0, 32), 4096.0,
                    None, op0=ALU.mult), w=[("AC", c_ac)], i=[("VE", 1)])
            barrier()
            for qt in range(16):
                c_pe = tr(PSM[:, 64:96], pf(PPZ + qt * 128, 128, 0, 32), IDF[0:32, 0:32],
                          w=[("VE", P.val("VE"))], i=[("PE", 1)])
                c = vstep(lambda E: E.reciprocal(pf(PRC, 16), PSM[:, 80:96]),
                          w=[("PE", c_pe)], i=[("VE", 1)])
                c = vstep(lambda E, qt=qt: E.tensor_scalar(
                    pf(PUW + qt * 32, 16), pf(PRC, 16), 4.0 * SQ8, None, op0=ALU.mult),
                    w=[("VE", c)], i=[("VE", 1)])
                c = vstep(lambda E, qt=qt: E.tensor_tensor(
                    pf(PUW + qt * 32 + 16, 16), pf(PUW + qt * 32, 16), PSM[:, 64:80],
                    op=ALU.mult), w=[("VE", c)], i=[("VE", 1)])
            dio(PFt[:, 0:4096], di["Jrow"][:, :])
            barrier()
            attn_core(False)
            barrier()
            proj_fm(di["wo"], 0, 8, post_accum_x(BBO))
            barrier()

        phase_attn_a(lv=min(nphases, 4))
        if nphases >= 5:
            phase_ffn()
        if nphases >= 6:
            phase_attn_b()
        if nphases >= 7:
            phase_ffn()

        for c in range(8):
            dio(y[c * 128:(c + 1) * 128, :], XR[:, c * 2048:(c + 1) * 2048],
                w=[("VE", P.val("VE"))] if c == 0 else ())
        for (tname, pool, p0, np_, c0, ncols) in debug_taps:
            src = {"PB": PBt, "PF": PFt, "XR": XR, "CONSTB": CONSTB}[pool]
            if pool in ("PB", "CONSTB"):
                dwl(taps[tname][:, :], src[p0:p0 + np_, c0:c0 + ncols])
            else:
                dio(taps[tname][:, :], src[p0:p0 + np_, c0:c0 + ncols])
        P.step("sync", None, [("IO", P.val("IO")), ("WL", P.val("WL"))])

        with nc.Block() as blk:
            @blk.sync
            def _(E):
                for f in P.steps["sync"]:
                    f(E)

            @blk.scalar
            def _(E):
                for f in P.steps["scalar"]:
                    f(E)

            @blk.vector
            def _(E):
                for f in P.steps["vector"]:
                    f(E)

            @blk.tensor
            def _(E):
                for f in P.steps["tensor"]:
                    f(E)

            @blk.gpsimd
            def _(E):
                for f in P.steps["gpsimd"]:
                    f(E)
    nc._plan = P
    return nc, di, taps


def kernel(**inputs):
    cores = make_core_inputs(inputs)
    nc, di, taps = build_nc()
    res = run_bass_kernel_spmd(nc, cores, core_ids=list(range(8)))
    out = np.empty((4, 4096, 1024), np.float32)
    for cid in range(8):
        b, h = cid // 2, cid % 2
        out[b, h * 2048:(h + 1) * 2048, :] = np.asarray(res.results[cid]["y"]).T
    return out


# version 04
# speedup vs baseline: 1.0652x; 1.0652x over previous
import numpy as np
import ml_dtypes
from contextlib import ExitStack

import concourse.bass as bass
import concourse.mybir as mybir
from concourse.bass_utils import run_bass_kernel_spmd

F32 = mybir.dt.float32
BF16 = mybir.dt.bfloat16
AF = mybir.ActivationFunctionType
ALU = mybir.AluOpType
AX = mybir.AxisListType
BF = ml_dtypes.bfloat16

EPS32 = float(np.finfo(np.float32).eps)
SQ8 = float(np.sqrt(8.0))

PB_N = 53248
PF_N = 8192
OH = 0
OQ = 16384
OKV = 32768
OWA = 40960
OWS = 49152
OLNR = OQ + 15360
PJ = 0
PWK = 4096
PPZ = 5120
PROW0 = 7168
PROW1 = 7680
PL0 = 4096
PL1 = 4112
PGLO = 4128
PMXS = 4140
PGLA = 4160
PGLB = 4288
PGLC = 4416
PMXQ = 4456
PMXK = 4500
PRC = 4544
PTMP = 4560
PCEPS1 = 4592
PCEPS2 = 4593
PUW = 4608
BNW, BBQ, BBK, BBO, BBFQ, BBFG, BGB = 0, 8, 16, 24, 32, 40, 48
BNEGB, BLI0, BLI1, BLS0, BLS1, BDEN = 56, 57, 58, 59, 60, 61

CCG = [[0, 1], [2, 3], [4, 5], [6, 7]]
ENGINES = ("sync", "scalar", "vector", "tensor", "gpsimd")


def n_live(qt):
    return min(8, (qt * 128 + 2048 + 132 + 511) // 512)


class Plan:
    def __init__(self):
        self.steps = {e: [] for e in ENGINES}
        self.cnt = {s: 0 for s in ("WL", "IO", "PE", "VE", "AC", "PO", "CC")}
        self.sems = {}
        self.log = []

    def val(self, name):
        return self.cnt[name]

    def step(self, eng, emitfn, w=(), i=()):
        for s, v in w:
            assert v <= self.cnt[s], f"[{eng}] wait {s}>={v} but only {self.cnt[s]} planned"
        last = None
        for s, a in i:
            self.cnt[s] += a
            last = self.cnt[s]
        self.log.append((eng, tuple(w), tuple((s, self.cnt[s]) for s, a in i)))
        sems = self.sems
        w = tuple(w)
        i = tuple(i)

        def run(E, w=w, i=i, emitfn=emitfn):
            for s, v in w:
                E.wait_ge(sems[s], v)
            inst = None
            if emitfn is not None:
                inst = emitfn(E)
            for s, a in i:
                assert inst is not None, "inc without instruction"
                inst = inst.then_inc(sems[s], a)
        self.steps[eng].append(run)
        return last


def _bcol(v):
    return np.ascontiguousarray(np.asarray(v, np.float32).reshape(8, 128))


def make_core_inputs(inputs):
    f = np.float32
    sh = {}
    sh["wqkvT"] = np.ascontiguousarray(np.asarray(inputs["L0_attn_a_w_qkv"], f).T)
    sh["woutT"] = np.ascontiguousarray(np.asarray(inputs["L0_attn_a_w_out"], f).T)
    sh["nw8"] = _bcol(inputs["L0_attn_a_norm_weight"])
    pm = np.asarray(inputs["L0_attn_a_persist_mem"], f)
    pmk = np.zeros((128, 32), f)
    pmv = np.zeros((4, 1024), f)
    for hpi in range(8):
        for hd in range(2):
            h_ = 2 * hpi + hd
            pmk[hd * 64:(hd + 1) * 64, hpi * 4:hpi * 4 + 4] = pm[h_].T
            pmv[:, hpi * 128 + hd * 64: hpi * 128 + (hd + 1) * 64] = pm[h_]
    sh["pmk"] = pmk.astype(BF)
    sh["pmv"] = pmv.astype(BF)
    for k in ("wq", "wk", "wv", "wo", "wfq", "wfg"):
        sh[k] = np.ascontiguousarray(np.asarray(inputs[f"L0_attn_b_{k}"], f))
    for k in ("bq", "bk", "bo", "bfq", "bfg"):
        sh[k + "8"] = _bcol(inputs[f"L0_attn_b_{k}"])
    sh["bv_row"] = np.asarray(inputs["L0_attn_b_bv"], f).reshape(1, 1024).astype(BF)
    up = np.asarray(inputs["L0_attn_b_up"], f)
    uz = np.asarray(inputs["L0_attn_b_uz"], f)
    U = np.zeros((8, 128, 32), f)
    for c in range(8):
        for r in range(2):
            h_ = 2 * c + r
            U[c, r * 64:(r + 1) * 64, h_] = up[h_, 0]
            U[c, r * 64:(r + 1) * 64, 16 + h_] = uz[h_, 0]
    sh["U"] = np.ascontiguousarray(U.reshape(1024, 32)).astype(BF)
    sh["sgT"] = np.ascontiguousarray(np.asarray(inputs["L0_ffn_sg"], f).T)
    sh["suT"] = np.ascontiguousarray(np.asarray(inputs["L0_ffn_su"], f).T)
    sh["sdT"] = np.ascontiguousarray(np.asarray(inputs["L0_ffn_sd"], f).T)
    sh["rgT"] = np.ascontiguousarray(np.transpose(np.asarray(inputs["L0_ffn_rg"], f), (0, 2, 1)))
    sh["ruT"] = np.ascontiguousarray(np.transpose(np.asarray(inputs["L0_ffn_ru"], f), (0, 2, 1)))
    sh["rdT"] = np.ascontiguousarray(np.transpose(np.asarray(inputs["L0_ffn_rd"], f), (0, 2, 1)))
    sh["wrT"] = np.ascontiguousarray(np.asarray(inputs["L0_ffn_wr"], f).T)
    sh["rb_row"] = np.asarray(inputs["L0_ffn_rb"], f).reshape(1, 7).astype(BF)

    sh["identb"] = np.eye(128).astype(BF)
    sh["identf"] = np.eye(128, dtype=f)
    sh["negidentb"] = (-np.eye(128)).astype(BF)
    perm = np.zeros((128, 128), f)
    for i in range(64):
        perm[2 * i + 1, 2 * i] = 1.0
        perm[2 * i, 2 * i + 1] = 1.0
    sh["permb"] = perm.astype(BF)
    sh["onesb"] = np.ones((128, 128), BF)
    canon = np.zeros((128, 6 * 512), f)
    rr = np.arange(128)[:, None]
    cc = np.arange(512)[None, :]
    for t in range(4):
        canon[:, (1 + t) * 512:(2 + t) * 512] = np.where(cc <= t * 128 + rr, 0.0, -1.0e9)
    canon[:, 5 * 512:] = -1.0e9
    sh["canon"] = canon.astype(BF)
    sh["Jrow"] = np.broadcast_to(np.arange(4096, dtype=f), (128, 4096)).copy()

    cores = []
    x = np.asarray(inputs["L0_attn_a_x"], f)
    inv_freq = 1.0 / (10000.0 ** (np.arange(0, 64, 2, dtype=np.float64) / 64))
    for cid in range(8):
        b, h = cid // 2, cid % 2
        d = dict(sh)
        d["xT"] = np.ascontiguousarray(x[b, h * 2048:(h + 1) * 2048, :].T)
        pos = np.arange(h * 2048, (h + 1) * 2048, dtype=np.float64)
        fr = np.repeat(pos[None, :] * inv_freq[:, None], 2, axis=0)
        cos = np.cos(fr)
        sgn = np.where((np.arange(64) % 2) == 0, -1.0, 1.0)[:, None]
        sin = np.sin(fr) * sgn
        d["cosT"] = np.ascontiguousarray(np.tile(cos, (2, 1)).astype(f))
        d["sinT"] = np.ascontiguousarray(np.tile(sin, (2, 1)).astype(f))
        canon6 = canon.astype(BF).reshape(128, 6, 512).transpose(1, 0, 2)
        th = 4 * h
        sel = np.zeros((16, 8), np.int64)
        for qt in range(16):
            t_ = th + qt // 4
            for ci in range(8):
                sel[qt, ci] = 0 if ci < t_ else ((1 + qt % 4) if ci == t_ else 5)
        d["masksel"] = np.ascontiguousarray(
            canon6[sel.reshape(-1)].reshape(16 * 8 * 128, 512))
        cores.append(d)
    return cores


def build_nc(debug_taps=(), nphases=8, core_lim=(8, 16), core_stage=4):
    nc = bass.Bass("TRN2", target_bir_lowering=False)
    di = {}

    def inp(name, shape, dt=F32):
        di[name] = nc.dram_tensor(name, list(shape), dt, kind="ExternalInput")
        return di[name]

    inp("xT", (1024, 2048))
    inp("wqkvT", (1024, 3072)); inp("woutT", (1024, 1024)); inp("nw8", (8, 128))
    inp("pmk", (128, 32), BF16); inp("pmv", (4, 1024), BF16)
    for k in ("wq", "wk", "wv", "wo", "wfq", "wfg"):
        inp(k, (1024, 1024))
    for k in ("bq8", "bk8", "bo8", "bfq8", "bfg8"):
        inp(k, (8, 128))
    inp("bv_row", (1, 1024), BF16); inp("U", (1024, 32), BF16)
    inp("sgT", (1024, 2816)); inp("suT", (1024, 2816)); inp("sdT", (2816, 1024))
    inp("rgT", (7, 1024, 2816)); inp("ruT", (7, 1024, 2816)); inp("rdT", (7, 2816, 1024))
    inp("wrT", (1024, 7)); inp("rb_row", (1, 7), BF16)
    inp("identb", (128, 128), BF16); inp("identf", (128, 128)); inp("negidentb", (128, 128), BF16)
    inp("permb", (128, 128), BF16); inp("onesb", (128, 128), BF16)
    inp("canon", (128, 6 * 512), BF16); inp("Jrow", (128, 4096))
    inp("masksel", (16384, 512), BF16)
    inp("cosT", (128, 2048)); inp("sinT", (128, 2048))

    y = nc.dram_tensor("y", [1024, 2048], F32, kind="ExternalOutput")
    cc_k_in = nc.dram_tensor("cc_k_in", [1026, 2048], BF16)
    cc_k_out = nc.dram_tensor("cc_k_out", [2052, 2048], BF16)
    ccvin = nc.dram_tensor("ccvin", [2048, 1024], BF16)
    ccvout = nc.dram_tensor("ccvout", [4096, 1024], BF16)

    taps = {}
    for (tname, pool, p0, np_, c0, ncols) in debug_taps:
        taps[tname] = nc.dram_tensor("tap_" + tname, [np_, ncols], F32, kind="ExternalOutput")

    P = Plan()
    REGS = {}
    with ExitStack() as ST:
        E_ = ST.enter_context
        PBt = E_(nc.sbuf_tensor("PB", [128, PB_N], BF16))
        PFt = E_(nc.sbuf_tensor("PF", [128, PF_N], F32))
        XR = E_(nc.sbuf_tensor("XR", [128, 16384], F32))
        CONSTB = E_(nc.sbuf_tensor("CONSTB", [128, 512], BF16))
        IDF = E_(nc.sbuf_tensor("IDF", [128, 128], F32))
        PMKt = E_(nc.sbuf_tensor("PMK", [128, 32], BF16))
        PMVt = E_(nc.sbuf_tensor("PMV", [4, 1024], BF16))
        UBt = E_(nc.sbuf_tensor("UB", [128, 256], BF16))
        BANKS = E_(nc.sbuf_tensor("BANKS", [128, 64], F32))
        ROWSB = E_(nc.sbuf_tensor("ROWSB", [1, 1036], BF16))
        V8 = E_(nc.sbuf_tensor("V8", [8, 128], F32))
        PS = [E_(nc.psum_tensor(f"PS{i}", [128, 512], F32)) for i in range(6)]
        PST = E_(nc.psum_tensor("PST", [128, 1024], BF16))
        PSM = E_(nc.psum_tensor("PSM", [128, 256], F32))

        for s in ("WL", "IO", "PE", "VE", "AC", "PO", "CC"):
            P.sems[s] = E_(nc.semaphore(s))

        IDB = CONSTB[:, 0:128]
        NIDB = CONSTB[:, 128:256]
        PERM = CONSTB[:, 256:384]
        ONES = CONSTB[:, 384:512]

        def pb(c0, n, p0=0, np_=128):
            return PBt[p0:p0 + np_, c0:c0 + n]

        def pf(c0, n, p0=0, np_=128):
            return PFt[p0:p0 + np_, c0:c0 + n]

        def src3(ap):
            return ap.rearrange("(k p) o -> p k o", p=128)

        def dst3(col, nk, ncols):
            return pb(col, nk * ncols).rearrange("p (k o) -> p k o", k=nk)

        def mm(dst, lh, rh, start, stop, w=(), i=()):
            return P.step("tensor", lambda E, a=dst, b=lh, c=rh, s=start, t=stop:
                          E.matmul(a, b, c, start=s, stop=t), w, i)

        def tr(dst, src, ident, w=(), i=()):
            return P.step("tensor", lambda E, a=dst, b=src, c=ident: E.transpose(a, b, c), w, i)

        def trg(dst, src, ident, start, stop, w=(), i=()):
            return P.step("tensor", lambda E, a=dst, b=src, c=ident, s=start, t=stop:
                          E.matmul(a, b, c, start=s, stop=t, is_transpose=True), w, i)

        def vstep(fn, w=(), i=()):
            return P.step("vector", fn, w, i)

        def astep(fn, w=(), i=()):
            return P.step("scalar", fn, w, i)

        def gstep(fn, w=(), i=()):
            return P.step("gpsimd", fn, w, i)

        def dio(out, in_, w=(), **kw):
            return P.step("sync", lambda E, a=out, b=in_, kw=kw: E.dma_start(out=a, in_=b, **kw),
                          w, [("IO", 16)])

        def dwl(out, in_, w=(), **kw):
            return P.step("gpsimd", lambda E, a=out, b=in_, kw=kw: E.dma_start(out=a, in_=b, **kw),
                          w, [("WL", 16)])

        def barrier():
            ws = [(s, P.val(s)) for s in ("PE", "VE", "AC", "PO", "WL", "IO", "CC")]
            for e in ENGINES:
                P.step(e, None, ws)

        dio(XR[:, :].rearrange("p (a t) -> p a t", a=8), di["xT"][:, :].rearrange("(a p) t -> p a t", p=128))
        dio(CONSTB[:, 0:128], di["identb"][:, :])
        dio(CONSTB[:, 128:256], di["negidentb"][:, :])
        dio(CONSTB[:, 256:384], di["permb"][:, :])
        dio(CONSTB[:, 384:512], di["onesb"][:, :])
        dio(IDF[:, :], di["identf"][:, :])
        dio(PMKt[:, :], di["pmk"][:, :])
        dio(PMVt[:, :], di["pmv"][:, :])
        for c in range(8):
            dio(UBt[:, c * 32:(c + 1) * 32], di["U"][c * 128:(c + 1) * 128, :])
        dio(ROWSB[0:1, 0:1024], di["bv_row"][:, :])
        dio(ROWSB[0:1, 1024:1031], di["rb_row"][:, :])
        dio(PFt[:, 0:2048], di["cosT"][:, :])
        c_setup = dio(PFt[:, 2048:4096], di["sinT"][:, :])
        vstep(lambda E: E.memset(pf(PCEPS1, 1, 0, 1), EPS32 * 1e-5))
        vstep(lambda E: E.memset(pf(PCEPS2, 1, 0, 1), 1e-5), i=[("VE", 1)])
        for vn, bc in (("nw8", BNW), ("bq8", BBQ), ("bk8", BBK), ("bo8", BBO),
                       ("bfq8", BBFQ), ("bfg8", BBFG)):
            c_io = dio(V8[:, :], di[vn][:, :], w=[("VE", P.val("VE"))])
            c_pe = tr(PSM[:, 0:8], V8[0:8, 0:128], IDF[0:8, 0:8], w=[("IO", c_io)], i=[("PE", 1)])
            vstep(lambda E, bc=bc: E.tensor_copy(BANKS[:, bc:bc + 8], PSM[:, 0:8]),
                  w=[("PE", c_pe)], i=[("VE", 1)])
        barrier()

        def ln_phase(which):
            xb = [OWS + 0, OWS + 512]
            xq = [OWS + 1024, OWS + 1536]
            for T in range(4):
                t0 = T * 512
                pe_done = []
                for c in range(8):
                    w = [("PE", pe_done[-2])] if len(pe_done) >= 2 else \
                        ([("PE", P.val("PE"))] if (T > 0 and c == 0) else [])
                    vstep(lambda E, c=c, t0=t0, s=xb[c % 2]: E.tensor_copy(
                        pb(s, 512), XR[:, c * 2048 + t0: c * 2048 + t0 + 512]), w=w)
                    c_ve = vstep(lambda E, c=c, t0=t0, s=xq[c % 2]: E.tensor_tensor(
                        pb(s, 512), XR[:, c * 2048 + t0: c * 2048 + t0 + 512],
                        XR[:, c * 2048 + t0: c * 2048 + t0 + 512], op=ALU.mult), i=[("VE", 1)])
                    mm(PS[0][0:1, :], ONES[:, 0:1], pb(xb[c % 2], 512), c == 0, c == 7,
                       w=[("VE", c_ve)])
                    c_pe = mm(PS[1][0:1, :], ONES[:, 0:1], pb(xq[c % 2], 512), c == 0, c == 7,
                              i=[("PE", 1)])
                    pe_done.append(c_pe)
                RM = pf(PROW0, 512, 0, 1)
                RV = pf(PROW1, 512, 0, 1)
                RT = pf(PWK, 512, 0, 1)
                vstep(lambda E: E.tensor_scalar(RM, PS[0][0:1, :], 1.0 / 1024.0, None, op0=ALU.mult),
                      w=[("PE", pe_done[-1])])
                vstep(lambda E: E.tensor_scalar(RV, PS[1][0:1, :], 1.0 / 1024.0, None, op0=ALU.mult))
                vstep(lambda E: E.tensor_tensor(RT, RM, RM, op=ALU.mult))
                c_ve = vstep(lambda E: E.tensor_tensor(RV, RV, RT, op=ALU.subtract), i=[("VE", 1)])
                lnsc = (1.0 + EPS32) if which == "rms" else 1.0
                lnbc = PCEPS1 if which == "rms" else PCEPS2
                astep(lambda E, s=lnsc, b=lnbc: E.activation(RT, RV, AF.Ln,
                                                             bias=pf(b, 1, 0, 1), scale=s),
                      w=[("VE", c_ve)])
                c_ac = astep(lambda E: E.activation(RV, RT, AF.Exp, scale=-0.5), i=[("AC", 1)])
                c_ve = vstep(lambda E: E.scalar_tensor_tensor(RT, RM, -1.0, RV,
                                                              op0=ALU.mult, op1=ALU.mult),
                             w=[("AC", c_ac)], i=[("VE", 1)])
                R1B = pb(OLNR, 512, 0, 1)
                R0B = pb(OLNR + 512, 512, 0, 1)
                vstep(lambda E: E.tensor_copy(R1B, RV), w=[("VE", c_ve)])
                c_ve = vstep(lambda E: E.tensor_copy(R0B, RT), i=[("VE", 1)])
                mm(PS[4][:, :], ONES[0:1, :], R1B, True, True,
                   w=[("VE", c_ve), ("VE", P.val("VE"))])
                c_pe = mm(PS[5][:, :], ONES[0:1, :], R0B, True, True, i=[("PE", 1)])
                TT = pf(PWK, 512)
                UT = pf(PWK + 512, 512)
                for c in range(8):
                    w = [("PE", c_pe)] if c == 0 else []
                    vstep(lambda E, c=c, t0=t0: E.tensor_tensor(
                        TT, XR[:, c * 2048 + t0: c * 2048 + t0 + 512], PS[4][:, :], op=ALU.mult), w=w)
                    if which == "rms":
                        vstep(lambda E: E.tensor_tensor(UT, TT, PS[5][:, :], op=ALU.add))
                        vstep(lambda E, c=c, t0=t0: E.tensor_scalar(
                            pb(OH + c * 2048 + t0, 512), UT, BANKS[:, BNW + c:BNW + c + 1],
                            None, op0=ALU.mult), i=[("VE", 1)])
                    else:
                        vstep(lambda E, c=c, t0=t0: E.tensor_tensor(
                            pb(OH + c * 2048 + t0, 512), TT, PS[5][:, :], op=ALU.add), i=[("VE", 1)])

        def proj_fm(wsrc, wcol0, nco, post, rhs_off=OH, slotA=OWA, slotB=OWA + 1024):
            slots = [slotA, slotB]
            slot_free = [(), ()]
            ps_free = {0: 0, 1: 0}
            last_ve = 0
            for o in range(nco):
                sl = slots[o % 2]
                c_wl = dwl(dst3(sl, 8, 128),
                           src3(wsrc[:, wcol0 + o * 128: wcol0 + (o + 1) * 128]),
                           w=slot_free[o % 2])
                pes = []
                for T in range(4):
                    psS = PS[T % 2]
                    w0 = [("WL", c_wl), ("VE", ps_free[T % 2])]
                    for kc in range(8):
                        c_pe = mm(psS[:, :], pb(sl + kc * 128, 128),
                                  pb(rhs_off + kc * 2048 + T * 512, 512),
                                  kc == 0, kc == 7,
                                  w=w0 if kc == 0 else (),
                                  i=[("PE", 1)] if kc == 7 else ())
                    pes.append(c_pe)
                    c_psfree, last_ve = post(o, T, psS, c_pe)
                    ps_free[T % 2] = c_psfree
                slot_free[o % 2] = [("PE", pes[-1])]
            return last_ve

        def post_plain(dst_off, bias_bank=None):
            def post(o, T, psS, c_pe):
                if bias_bank is not None:
                    c_ve = vstep(lambda E, o=o, T=T, psS=psS: E.tensor_scalar(
                        pb(dst_off + o * 2048 + T * 512, 512), psS[:, :],
                        BANKS[:, bias_bank + o:bias_bank + o + 1], None, op0=ALU.add),
                        w=[("PE", c_pe)], i=[("VE", 1)])
                else:
                    c_ve = vstep(lambda E, o=o, T=T, psS=psS: E.tensor_copy(
                        pb(dst_off + o * 2048 + T * 512, 512), psS[:, :]),
                        w=[("PE", c_pe)], i=[("VE", 1)])
                return c_ve, c_ve
            return post

        def post_accum_x(bias_bank=None):
            def post(o, T, psS, c_pe):
                xsl = XR[:, o * 2048 + T * 512: o * 2048 + (T + 1) * 512]
                if bias_bank is not None:
                    c_ve = vstep(lambda E, o=o, psS=psS, xsl=xsl: E.scalar_tensor_tensor(
                        xsl, psS[:, :], BANKS[:, bias_bank + o:bias_bank + o + 1], xsl,
                        op0=ALU.add, op1=ALU.add), w=[("PE", c_pe)], i=[("VE", 1)])
                else:
                    c_ve = vstep(lambda E, psS=psS, xsl=xsl: E.tensor_tensor(
                        xsl, xsl, psS[:, :], op=ALU.add), w=[("PE", c_pe)], i=[("VE", 1)])
                return c_ve, c_ve
            return post

        def mk_post_rope(dst_off, stage_rows=None, maxbase=None, maxcells=None):
            st = {"psR": [0, 0], "stg": [0, 0], "msr": [0, 0], "cell": 0}

            def post(o, T, psS, c_pe):
                kraw = OWS + 2048 + (T % 2) * 512
                c_ve1 = vstep(lambda E, kraw=kraw, psS=psS: E.tensor_copy(pb(kraw, 512), psS[:, :]),
                              w=[("PE", c_pe)], i=[("VE", 1)])
                psR = PS[2 + T % 2]
                c_pe2 = mm(psR[:, :], PERM, pb(kraw, 512), True, True,
                           w=[("VE", max(c_ve1, st["psR"][T % 2]))], i=[("PE", 1)])
                TT = pf(PWK, 512)
                UT = pf(PWK + 512, 512)
                c_rot = vstep(lambda E, T=T, psR=psR: E.tensor_tensor(
                    TT, psR[:, :], PFt[:, 2048 + T * 512: 2048 + (T + 1) * 512], op=ALU.mult),
                    w=[("PE", c_pe2)], i=[("VE", 1)])
                st["psR"][T % 2] = c_rot
                vstep(lambda E, kraw=kraw, T=T: E.tensor_tensor(
                    UT, pb(kraw, 512), PFt[:, T * 512:(T + 1) * 512], op=ALU.mult))
                if dst_off is not None:
                    dst = dst_off + o * 2048 + T * 512
                else:
                    dst = OWS + 3072 + (T % 2) * 512
                wext = [("IO", st["stg"][T % 2])] if (stage_rows is not None and dst_off is None) else []
                c_ve = vstep(lambda E, dst=dst: E.tensor_tensor(pb(dst, 512), UT, TT, op=ALU.add),
                             w=wext, i=[("VE", 1)])
                if stage_rows is not None:
                    dram, rowbase = stage_rows
                    c_io = dio(dram[rowbase + o * 128: rowbase + (o + 1) * 128,
                                    T * 512:(T + 1) * 512], pb(dst, 512), w=[("VE", c_ve)])
                    st["stg"][T % 2] = c_io
                if maxbase is not None:
                    SQ = pb(OWS + (T % 2) * 512, 512)
                    c_sq = vstep(lambda E, dst=dst, SQ=SQ: E.tensor_tensor(
                        SQ, pb(dst, 512), pb(dst, 512), op=ALU.mult), i=[("VE", 1)])
                    psROW = PS[4 + T % 2]
                    c_pe3 = mm(psROW[0:1, :], ONES[:, 0:1], SQ,
                               True, True, w=[("VE", max(c_sq, st["msr"][T % 2]))], i=[("PE", 1)])
                    cell = st["cell"]
                    st["cell"] += 1
                    c_red = vstep(lambda E, cell=cell, psROW=psROW: E.tensor_reduce(
                        pf(maxbase + cell, 1, 0, 1), psROW[0:1, :], axis=AX.X, op=ALU.max),
                        w=[("PE", c_pe3)], i=[("VE", 1)])
                    st["msr"][T % 2] = c_red
                    maxcells[0] = max(maxcells[0], st["cell"])
                return c_ve1, c_ve
            return post

        def post_stage_bias(bias_bank, stage_rows, maxbase=None, maxcells=None):
            st = {"stg": [0, 0], "msr": [0, 0], "cell": 0}

            def post(o, T, psS, c_pe):
                dst = OWS + 3072 + (T % 2) * 512
                c_ve = vstep(lambda E, o=o, dst=dst, psS=psS: E.tensor_scalar(
                    pb(dst, 512), psS[:, :], BANKS[:, bias_bank + o: bias_bank + o + 1],
                    None, op0=ALU.add), w=[("PE", c_pe), ("IO", st["stg"][T % 2])], i=[("VE", 1)])
                dram, rowbase = stage_rows
                c_io = dio(dram[rowbase + o * 128: rowbase + (o + 1) * 128,
                                T * 512:(T + 1) * 512], pb(dst, 512), w=[("VE", c_ve)])
                st["stg"][T % 2] = c_io
                if maxbase is not None:
                    SQ = pb(OWS + (T % 2) * 512, 512)
                    c_sq = vstep(lambda E, dst=dst, SQ=SQ: E.tensor_tensor(
                        SQ, pb(dst, 512), pb(dst, 512), op=ALU.mult), i=[("VE", 1)])
                    psROW = PS[4 + T % 2]
                    c_pe3 = mm(psROW[0:1, :], ONES[:, 0:1], SQ, True, True,
                               w=[("VE", max(c_sq, st["msr"][T % 2]))], i=[("PE", 1)])
                    cell = st["cell"]
                    st["cell"] += 1
                    c_red = vstep(lambda E, cell=cell, psROW=psROW: E.tensor_reduce(
                        pf(maxbase + cell, 1, 0, 1), psROW[0:1, :], axis=AX.X, op=ALU.max),
                        w=[("PE", c_pe3)], i=[("VE", 1)])
                    st["msr"][T % 2] = c_red
                    maxcells[0] = max(maxcells[0], st["cell"])
                return c_ve, c_ve
            return post

        def proj_v_nobias(wsrc, wcol0, vslot=OWA + 2048):
            c_prev_pe = P.val("PE")
            c_stage_io = [0, 0]
            for vo in range(2):
                c_wl = dwl(dst3(vslot, 8, 512),
                           src3(wsrc[:, wcol0 + vo * 512: wcol0 + (vo + 1) * 512]),
                           w=[("PE", c_prev_pe)])
                for t16 in range(16):
                    psS = PS[t16 % 2]
                    w0 = [("WL", c_wl), ("VE", P.val("VE") if t16 >= 2 else 0)]
                    for kc in range(8):
                        c_pe = mm(psS[:, :], pb(OH + kc * 2048 + t16 * 128, 128),
                                  pb(vslot + kc * 512, 512), kc == 0, kc == 7,
                                  w=w0 if kc == 0 else (), i=[("PE", 1)] if kc == 7 else ())
                    dst = OWS + 3072 + (t16 % 2) * 512
                    c_ve = vstep(lambda E, dst=dst, psS=psS: E.tensor_copy(pb(dst, 512), psS[:, :]),
                                 w=[("PE", c_pe), ("IO", c_stage_io[t16 % 2])], i=[("VE", 1)])
                    c_io = dio(ccvin[t16 * 128:(t16 + 1) * 128, vo * 512:(vo + 1) * 512],
                               pb(dst, 512), w=[("VE", c_ve)])
                    c_stage_io[t16 % 2] = c_io
                    c_prev_pe = c_pe

        def proj_v_bias(wsrc, wcol0, vslot=OWA + 2048):
            c_prev_pe = P.val("PE")
            c_stage_io = [0, 0]
            for vo in range(2):
                c_wl = dwl(dst3(vslot, 8, 512),
                           src3(wsrc[:, wcol0 + vo * 512: wcol0 + (vo + 1) * 512]),
                           w=[("PE", c_prev_pe)])
                for t16 in range(16):
                    psS = PS[t16 % 2]
                    w0 = [("WL", c_wl), ("VE", P.val("VE") if t16 >= 2 else 0)]
                    for kc in range(8):
                        mm(psS[:, :], pb(OH + kc * 2048 + t16 * 128, 128), pb(vslot + kc * 512, 512),
                           kc == 0, False, w=w0 if kc == 0 else ())
                    c_pe = mm(psS[:, :], ONES[0:1, :], ROWSB[0:1, vo * 512:(vo + 1) * 512],
                              False, True, i=[("PE", 1)])
                    dst = OWS + 3072 + (t16 % 2) * 512
                    c_ve = vstep(lambda E, dst=dst, psS=psS: E.tensor_copy(pb(dst, 512), psS[:, :]),
                                 w=[("PE", c_pe), ("IO", c_stage_io[t16 % 2])], i=[("VE", 1)])
                    c_io = dio(ccvin[t16 * 128:(t16 + 1) * 128, vo * 512:(vo + 1) * 512],
                               pb(dst, 512), w=[("VE", c_ve)])
                    c_stage_io[t16 % 2] = c_io
                    c_prev_pe = c_pe

        def finalize_max(maxbase, ncells, dstcell):
            c = vstep(lambda E, n=ncells: E.tensor_reduce(
                pf(dstcell, 1, 0, 1), pf(maxbase, ncells, 0, 1), axis=AX.X, op=ALU.max),
                i=[("VE", 1)])
            return c

        def make_negB(qcell, kcells, w=()):
            c = vstep(lambda E, k0=kcells[0]: E.tensor_copy(pf(PMXS, 1, 0, 1), pf(k0, 1, 0, 1)),
                      w=w, i=[("VE", 1)])
            for kc_ in kcells[1:]:
                c = vstep(lambda E, kc_=kc_: E.tensor_tensor(
                    pf(PMXS, 1, 0, 1), pf(PMXS, 1, 0, 1), pf(kc_, 1, 0, 1), op=ALU.max),
                    w=[("VE", c)], i=[("VE", 1)])
            c = vstep(lambda E: E.tensor_tensor(pf(PMXS, 1, 0, 1), pf(PMXS, 1, 0, 1),
                                                pf(qcell, 1, 0, 1), op=ALU.add),
                      w=[("VE", c)], i=[("VE", 1)])
            c = vstep(lambda E: E.tensor_scalar(pf(PMXS, 1, 0, 1), pf(PMXS, 1, 0, 1),
                                                -1.0 / 16.0, None, op0=ALU.mult),
                      w=[("VE", c)], i=[("VE", 1)])
            c = vstep(lambda E: E.tensor_copy(ROWSB[0:1, 1031:1032], pf(PMXS, 1, 0, 1)),
                      w=[("VE", c)], i=[("VE", 1)])
            c_pe = mm(PSM[:, 16:17], ONES[0:1, :], ROWSB[0:1, 1031:1032], True, True,
                      w=[("VE", c)], i=[("PE", 1)])
            c = vstep(lambda E: E.tensor_copy(BANKS[:, BNEGB:BNEGB + 1], PSM[:, 16:17]),
                      w=[("PE", c_pe)], i=[("VE", 1)])
            return c

        def attn_core(is_a):
            nhp, nqt = core_lim
            Pslot = [OWS + 0, OWS + 512, OWS + 1024, OWS + 1536]
            PTs = [OWS + 2048, OWS + 2560]
            MSK = [OWS + 3072, OWS + 3584]
            T2INs = OWS + 3072
            T2Ns = OWS + 3584
            OSB = OWS + 0
            msk_io = [0, 0]
            msk_pe = [0, 0]
            st_ = {"kv_io": 0, "avpe": P.val("PE"), "ofve": P.val("VE"),
                   "t2pe": 0, "t2po": 0, "t2ve": 0}
            for hp in range(nhp):
                w_kv = [("CC", P.val("CC")), ("PE", st_["avpe"])]
                dio(pb(OKV, 2048), cc_k_out[hp * 128:(hp + 1) * 128, :], w=w_kv)
                dio(pb(OKV + 2048, 2048), cc_k_out[1026 + hp * 128:1026 + (hp + 1) * 128, :])
                for b8 in range(4):
                    st_["kv_io"] = dio(dst3(OKV + 4096 + b8 * 1024, 8, 128),
                                       src3(ccvout[b8 * 1024:(b8 + 1) * 1024,
                                                   hp * 128:(hp + 1) * 128]))
                for qt in range(nqt):
                    nl = n_live(qt) if is_a else 8
                    npm = 1 if is_a else 0
                    n_st = nl + npm
                    lbase = PL0 + (qt % 2) * 32
                    if is_a:
                        for cpre in range(min(2, nl)):
                            msk_io[cpre % 2] = dio(
                                pb(MSK[cpre % 2], 512),
                                di["masksel"][(qt * 8 + cpre) * 128:(qt * 8 + cpre + 1) * 128, :],
                                w=[("PE", msk_pe[cpre % 2])])
                    c_exp = {(h_, p_): P.val("AC") for h_ in (0, 1) for p_ in (0, 1)}
                    c_ptev = {0: P.val("VE"), 1: P.val("VE")}

                    def emit_scores(ci, qt=qt, nl=nl, lbase=lbase, hp=hp,
                                    c_exp=c_exp):
                        is_pm = is_a and (ci == nl)
                        for hd in range(2):
                            q_lh = pb(OQ + hp * 2048 + qt * 128, 128, hd * 64, 64)
                            if is_pm:
                                c_pe = mm(PSM[:, hd * 4:hd * 4 + 4], q_lh,
                                          PMKt[hd * 64:(hd + 1) * 64, hp * 4:hp * 4 + 4],
                                          True, True,
                                          w=[("AC", max(c_exp[(0, 0)], c_exp[(0, 1)])),
                                             ("AC", max(c_exp[(1, 0)], c_exp[(1, 1)]))],
                                          i=[("PE", 1)])
                                c_ac = astep(lambda E, hd=hd, ci=ci, lbase=lbase: E.activation(
                                    pb(Pslot[hd * 2 + ci % 2], 4), PSM[:, hd * 4:hd * 4 + 4],
                                    AF.Exp, bias=BANKS[:, BNEGB:BNEGB + 1], scale=0.125,
                                    accum_out=pf(lbase + hd * 16 + ci, 1)),
                                    w=[("PE", c_pe)], i=[("AC", 1)])
                            else:
                                psS = PS[hd + 2 * (ci % 2)]
                                wS = [("AC", c_exp[(hd, ci % 2)]), ("IO", st_["kv_io"])]
                                mm(psS[:, :], q_lh, pb(OKV + ci * 512, 512, hd * 64, 64),
                                   True, False, w=wS)
                                if is_a:
                                    c_pe = mm(psS[:, :], IDB, pb(MSK[ci % 2], 512),
                                              False, True,
                                              w=[("IO", msk_io[ci % 2])], i=[("PE", 1)])
                                    if hd == 1:
                                        msk_pe[ci % 2] = c_pe
                                        if ci + 2 < nl:
                                            msk_io[ci % 2] = dio(
                                                pb(MSK[ci % 2], 512),
                                                di["masksel"][(qt * 8 + ci + 2) * 128:
                                                              (qt * 8 + ci + 3) * 128, :],
                                                w=[("PE", c_pe)])
                                else:
                                    c_ve = vstep(lambda E, ci=ci, qt=qt, hd=hd: E.tensor_scalar(
                                        pb(T2INs, 512), PFt[:, ci * 512:(ci + 1) * 512],
                                        PFt[:, PUW + qt * 32 + hd:PUW + qt * 32 + hd + 1],
                                        PFt[:, PUW + qt * 32 + 16 + hd:PUW + qt * 32 + 16 + hd + 1],
                                        op0=ALU.mult, op1=ALU.subtract),
                                        w=[("PO", st_["t2po"])], i=[("VE", 1)])
                                    st_["t2ve"] = c_ve
                                    c_po = gstep(lambda E: E.tensor_tensor(
                                        pb(T2Ns, 512), pb(T2INs, 512), pb(T2INs, 512),
                                        op=ALU.mult),
                                        w=[("VE", c_ve), ("PE", st_["t2pe"])], i=[("PO", 1)])
                                    st_["t2po"] = c_po
                                    c_pe = mm(psS[:, :], NIDB, pb(T2Ns, 512), False, True,
                                              w=[("PO", c_po)], i=[("PE", 1)])
                                    st_["t2pe"] = c_pe
                                c_ac = astep(lambda E, hd=hd, ci=ci, psS=psS, lbase=lbase:
                                             E.activation(
                                                 pb(Pslot[hd * 2 + ci % 2], 512), psS[:, :],
                                                 AF.Exp, bias=BANKS[:, BNEGB:BNEGB + 1],
                                                 scale=0.125,
                                                 accum_out=pf(lbase + hd * 16 + ci, 1)),
                                             w=[("PE", c_pe)], i=[("AC", 1)])
                            c_exp[(hd, ci % 2)] = c_ac

                    def emit_tail(ci, qt=qt, nl=nl, hp=hp, c_exp=c_exp, c_ptev=c_ptev):
                        is_pm = is_a and (ci == nl)
                        nblk = 1 if is_pm else 4
                        for hd in range(2):
                            c_ac = c_exp[(hd, ci % 2)]
                            wtr = [("AC", c_ac), ("VE", c_ptev[0]), ("VE", c_ptev[1])]
                            ps_ = Pslot[hd * 2 + ci % 2]
                            if is_pm:
                                c_pe = trg(PST[0:4, hd * 512:hd * 512 + 128],
                                           pb(ps_, 4), IDB, True, True, w=wtr, i=[("PE", 1)])
                                c_ve = vstep(lambda E, hd=hd: E.tensor_copy(
                                    pb(PTs[hd], 128, 0, 4), PST[0:4, hd * 512:hd * 512 + 128]),
                                    w=[("PE", c_pe)], i=[("VE", 1)])
                            else:
                                for k in range(4):
                                    c_pe = trg(PST[:, hd * 512 + k * 128: hd * 512 + (k + 1) * 128],
                                               pb(ps_ + k * 128, 128), IDB, k == 0, k == 3,
                                               w=wtr if k == 0 else (),
                                               i=[("PE", 1)] if k == 3 else ())
                                c_ve = vstep(lambda E, hd=hd: E.tensor_copy(
                                    pb(PTs[hd], 512), PST[:, hd * 512:(hd + 1) * 512]),
                                    w=[("PE", c_pe)], i=[("VE", 1)])
                            c_ptev[hd] = c_ve
                            psAV = PS[4 + hd]
                            for k in range(nblk):
                                if is_pm:
                                    c_avpe = mm(psAV[:, 0:64], pb(PTs[hd], 128, 0, 4),
                                                PMVt[0:4, hp * 128 + hd * 64:
                                                     hp * 128 + (hd + 1) * 64],
                                                False, True, w=[("VE", c_ve)], i=[("PE", 1)])
                                else:
                                    blk = ci * 4 + k
                                    wav = [("VE", c_ve)] if k == 0 else []
                                    if ci == 0 and k == 0:
                                        wav.append(("VE", st_["ofve"]))
                                    c_avpe = mm(psAV[:, 0:64], pb(PTs[hd] + k * 128, 128),
                                                pb(OKV + 4096 + blk * 128 + hd * 64, 64),
                                                ci == 0 and k == 0,
                                                (not is_a) and ci == 7 and k == 3,
                                                w=wav, i=[("PE", 1)])
                        st_["avpe"] = c_avpe

                    for stg in range(n_st + 1):
                        if stg < n_st:
                            emit_scores(stg)
                        if stg >= 1:
                            emit_tail(stg - 1)
                    nls = n_st
                    c0 = vstep(lambda E, nls=nls, lbase=lbase: E.tensor_reduce(
                        BANKS[:, BLS0:BLS0 + 1], pf(lbase, nls), axis=AX.X, op=ALU.add),
                        w=[("AC", max(c_exp[(0, 0)], c_exp[(0, 1)])),
                           ("AC", max(c_exp[(1, 0)], c_exp[(1, 1)]))], i=[("VE", 1)])
                    c1 = vstep(lambda E, nls=nls, lbase=lbase: E.tensor_reduce(
                        BANKS[:, BLS1:BLS1 + 1], pf(lbase + 16, nls), axis=AX.X, op=ALU.add),
                        i=[("VE", 1)])
                    c0 = vstep(lambda E: E.reciprocal(BANKS[:, BLI0:BLI0 + 1],
                                                      BANKS[:, BLS0:BLS0 + 1]),
                               w=[("VE", c1)], i=[("VE", 1)])
                    c1 = vstep(lambda E: E.reciprocal(BANKS[:, BLI1:BLI1 + 1],
                                                      BANKS[:, BLS1:BLS1 + 1]),
                               w=[("VE", c0)], i=[("VE", 1)])
                    vstep(lambda E: E.tensor_scalar(pb(OSB, 64), PS[4][:, 0:64],
                                                    BANKS[:, BLI0:BLI0 + 1], None, op0=ALU.mult),
                          w=[("VE", c1), ("PE", st_["avpe"]), ("VE", c_ptev[0]),
                             ("VE", c_ptev[1])])
                    c_ve = vstep(lambda E: E.tensor_scalar(pb(OSB + 64, 64), PS[5][:, 0:64],
                                                           BANKS[:, BLI1:BLI1 + 1], None,
                                                           op0=ALU.mult),
                                 i=[("VE", 1)])
                    c_pe = trg(PST[:, 0:128], pb(OSB, 128), IDB, True, True,
                               w=[("VE", c_ve)], i=[("PE", 1)])
                    st_["ofve"] = vstep(lambda E, hp=hp, qt=qt: E.tensor_copy(
                        pb(OH + hp * 2048 + qt * 128, 128), PST[:, 0:128]),
                        w=[("PE", c_pe)], i=[("VE", 1)])

        def phase_attn_a(lv=4):
            ln_phase("rms")
            barrier()
            if lv < 2:
                return
            qmax = [0]
            kmax = [0]
            proj_fm(di["wqkvT"], 1024, 8,
                    mk_post_rope(None, (cc_k_in, 0), PMXK, kmax))
            proj_v_nobias(di["wqkvT"], 2048)
            c_ve = finalize_max(PMXK, kmax[0], PMXK + 38)
            dwl(cc_k_in[1025:1026, 0:1], pf(PMXK + 38, 1, 0, 1), w=[("VE", c_ve)])
            c_cc = gstep(lambda E: E.collective_compute(
                "AllGather", ALU.bypass, replica_groups=CCG,
                ins=[cc_k_in.ap()], outs=[cc_k_out.ap()]),
                w=[("IO", P.val("IO")), ("WL", P.val("WL"))], i=[("CC", 1)])
            c_cc = gstep(lambda E: E.collective_compute(
                "AllGather", ALU.bypass, replica_groups=CCG,
                ins=[ccvin.ap()], outs=[ccvout.ap()]), i=[("CC", 1)])
            proj_fm(di["wqkvT"], 0, 8, mk_post_rope(OQ, None, PMXQ, qmax))
            c_ve = finalize_max(PMXQ, qmax[0], PMXQ + 38)
            c_sq = vstep(lambda E: E.tensor_tensor(pb(OWS, 32), PMKt[:, :], PMKt[:, :], op=ALU.mult),
                         i=[("VE", 1)])
            c_pe = mm(PSM[0:1, 32:64], ONES[:, 0:1], pb(OWS, 32), True, True,
                      w=[("VE", c_sq)], i=[("PE", 1)])
            c_ve2 = vstep(lambda E: E.tensor_reduce(pf(PMXK + 39, 1, 0, 1), PSM[0:1, 32:64],
                                                    axis=AX.X, op=ALU.max),
                          w=[("PE", c_pe)], i=[("VE", 1)])
            barrier()
            dwl(pf(PMXK + 36, 1, 0, 1), cc_k_out[1025:1026, 0:1], w=[("CC", P.val("CC"))])
            c_wl = dwl(pf(PMXK + 37, 1, 0, 1), cc_k_out[2051:2052, 0:1])
            make_negB(PMXQ + 38, [PMXK + 36, PMXK + 37, PMXK + 39], w=[("WL", c_wl)])
            barrier()
            if lv < 3:
                return
            attn_core(True)
            barrier()
            if lv < 4:
                return
            proj_fm(di["woutT"], 0, 8, post_accum_x(None))
            barrier()

        def phase_ffn():
            ln_phase("plain")
            barrier()
            WRS = OWS + 3952
            c_wl = dwl(dst3(WRS, 8, 7), src3(di["wrT"][:, :]))
            WSM = OWS + 3840
            for t16 in range(16):
                w0 = [("WL", c_wl)] if t16 == 0 else [("VE", P.val("VE"))]
                for kc in range(8):
                    mm(PSM[:, 0:7], pb(OH + kc * 2048 + t16 * 128, 128),
                       pb(WRS + kc * 7, 7), kc == 0, False, w=w0 if kc == 0 else ())
                c_pe = mm(PSM[:, 0:7], ONES[0:1, :], ROWSB[0:1, 1024:1031], False, True,
                          i=[("PE", 1)])
                EL = pf(PWK, 7)
                M1 = pf(PWK + 8, 1)
                EE = pf(PWK + 16, 7)
                EQ = pf(PWK + 24, 7)
                M2 = pf(PWK + 32, 1)
                GE = pf(PWK + 40, 7)
                DEN = pf(PWK + 48, 1)
                DIV = pf(PWK + 56, 1)
                WW = pf(PWK + 64, 7)
                c = vstep(lambda E: E.tensor_copy(EL, PSM[:, 0:7]), w=[("PE", c_pe)], i=[("VE", 1)])
                c = vstep(lambda E: E.tensor_reduce(M1, EL, axis=AX.X, op=ALU.max),
                          w=[("VE", c)], i=[("VE", 1)])
                c = vstep(lambda E: E.tensor_scalar(pf(PWK + 9, 1), M1, -1.0, None, op0=ALU.mult),
                          w=[("VE", c)], i=[("VE", 1)])
                c_ac = astep(lambda E: E.activation(EE, EL, AF.Exp, bias=pf(PWK + 9, 1), scale=1.0),
                             w=[("VE", c)], i=[("AC", 1)])
                c = vstep(lambda E: E.tensor_reduce(M1, EE, axis=AX.X, op=ALU.max),
                          w=[("AC", c_ac)], i=[("VE", 1)])
                c = vstep(lambda E: E.tensor_scalar(EQ, EE, M1, None, op0=ALU.is_equal),
                          w=[("VE", c)], i=[("VE", 1)])
                c = vstep(lambda E: E.scalar_tensor_tensor(EQ, EQ, M1, EE, op0=ALU.mult,
                                                           op1=ALU.subtract), w=[("VE", c)], i=[("VE", 1)])
                c = vstep(lambda E: E.tensor_reduce(M2, EQ, axis=AX.X, op=ALU.min),
                          w=[("VE", c)], i=[("VE", 1)])
                c = vstep(lambda E: E.tensor_scalar(M2, M2, -1.0, None, op0=ALU.mult),
                          w=[("VE", c)], i=[("VE", 1)])
                c = vstep(lambda E: E.tensor_scalar(GE, EE, M2, None, op0=ALU.is_ge),
                          w=[("VE", c)], i=[("VE", 1)])
                c = vstep(lambda E: E.tensor_scalar(DEN, M1, M2, None, op0=ALU.add),
                          w=[("VE", c)], i=[("VE", 1)])
                c = vstep(lambda E: E.reciprocal(DIV, DEN), w=[("VE", c)], i=[("VE", 1)])
                c = vstep(lambda E: E.scalar_tensor_tensor(WW, EE, DIV, GE, op0=ALU.mult,
                                                           op1=ALU.mult), w=[("VE", c)], i=[("VE", 1)])
                c = vstep(lambda E, t16=t16: E.tensor_copy(pb(WSM + t16 * 7, 7), WW),
                          w=[("VE", c)], i=[("VE", 1)])
            barrier()
            OIN = OQ
            WB = OWS + 2816
            WROW = OWS + 3328
            gs = [OKV, OKV + 2048]
            us = [OKV + 4096, OKV + 6144]
            ds = [OWA, OWA + 2816]
            c_gfree = [(), ()]
            c_ufree = [(), ()]
            c_dfree = [(), ()]
            c_silu = [0, 0]
            c_inn = [0, 0]
            c_dpe = [0, 0]
            c_gupe = [0, 0]
            for e in range(8):
                if e == 0:
                    gsrc, usrc, dsrc = di["sgT"][:, :], di["suT"][:, :], di["sdT"][:, :]
                else:
                    gsrc, usrc, dsrc = di["rgT"][e - 1], di["ruT"][e - 1], di["rdT"][e - 1]
                for T in range(4):
                    c_gwl = {}
                    c_uwl = {}
                    for j in range(2):
                        wslot = (c_gfree[j] if T == 0 and e == 0 else [("PE", c_gupe[j])])
                        c_gwl[j] = dwl(dst3(gs[j], 8, 256),
                                       src3(gsrc[:, j * 256:(j + 1) * 256]), w=wslot)
                        c_uwl[j] = dwl(dst3(us[j], 8, 256),
                                       src3(usrc[:, j * 256:(j + 1) * 256]))
                    c_dwl = {}
                    for o in range(2):
                        c_dwl[o] = dwl(dst3(ds[o], 22, 128),
                                       src3(dsrc[:, o * 128:(o + 1) * 128]),
                                       w=(c_dfree[o] if T == 0 and e == 0 else [("PE", c_dpe[o])]))
                    nexto = [2]
                    if e > 0:
                        c_pes = []
                        for jj in range(4):
                            t16 = T * 4 + jj
                            c_pes.append(trg(PST[0:1, jj * 128:(jj + 1) * 128],
                                             pb(WSM + t16 * 7 + (e - 1), 1), IDB,
                                             jj == 0, jj == 3,
                                             w=[("VE", P.val("VE"))] if jj == 0 else (),
                                             i=[("PE", 1)]))
                        c_ve = vstep(lambda E: E.tensor_copy(pb(WROW, 512, 0, 1), PST[0:1, 0:512]),
                                     w=[("PE", c_pes[-1])], i=[("VE", 1)])
                        c_pe = mm(PS[4][:, :], ONES[0:1, :], pb(WROW, 512, 0, 1), True, True,
                                  w=[("VE", c_ve), ("VE", P.val("VE"))], i=[("PE", 1)])
                        c_wbve = vstep(lambda E: E.tensor_copy(pb(WB, 512), PS[4][:, :]),
                                       w=[("PE", c_pe)], i=[("VE", 1)])
                    for ic in range(22):
                        j, which = divmod(ic, 2)
                        psG = PS[ic % 2]
                        psU = PS[2 + ic % 2]
                        wg = []
                        if which == 0:
                            wg.append(("WL", c_gwl[j]))
                        if c_silu[ic % 2]:
                            wg.append(("AC", c_silu[ic % 2]))
                        for kc in range(8):
                            mm(psG[:, :], pb(gs[j % 2] + kc * 256 + which * 128, 128),
                               pb(OH + kc * 2048 + T * 512, 512), kc == 0, kc == 7,
                               w=wg if kc == 0 else ())
                        wu = []
                        if which == 0:
                            wu.append(("WL", c_uwl[j]))
                        if c_inn[ic % 2]:
                            wu.append(("VE", c_inn[ic % 2]))
                        for kc in range(8):
                            c_pe = mm(psU[:, :], pb(us[j % 2] + kc * 256 + which * 128, 128),
                                      pb(OH + kc * 2048 + T * 512, 512), kc == 0, kc == 7,
                                      w=wu if kc == 0 else (),
                                      i=[("PE", 1)] if kc == 7 else ())
                        c_ac = astep(lambda E, ic=ic, psG=psG: E.activation(
                            pb(OIN + ic * 512, 512), psG[:, :], AF.Silu),
                            w=[("PE", c_pe)], i=[("AC", 1)])
                        c_silu[ic % 2] = c_ac
                        c_ve = vstep(lambda E, ic=ic, psU=psU: E.tensor_tensor(
                            pb(OIN + ic * 512, 512), pb(OIN + ic * 512, 512), psU[:, :],
                            op=ALU.mult), w=[("AC", c_ac)], i=[("VE", 1)])
                        c_inn[ic % 2] = c_ve
                        if which == 1:
                            c_gupe[j % 2] = c_pe
                            if j + 2 < 11:
                                jn = j + 2
                                c_gwl[jn] = dwl(dst3(gs[jn % 2], 8, 256),
                                                src3(gsrc[:, jn * 256:(jn + 1) * 256]),
                                                w=[("PE", c_pe)])
                                c_uwl[jn] = dwl(dst3(us[jn % 2], 8, 256),
                                                src3(usrc[:, jn * 256:(jn + 1) * 256]))
                    c_dve = [P.val("VE")] * 2
                    for o in range(8):
                        psD = PS[4 + o % 2]
                        wd = [("WL", c_dwl[o]), ("VE", c_inn[0]), ("VE", c_inn[1]),
                              ("VE", c_dve[o % 2])]
                        if e > 0:
                            wd.append(("VE", c_wbve))
                        for ic in range(22):
                            c_pe = mm(psD[:, :], pb(ds[o % 2] + ic * 128, 128),
                                      pb(OIN + ic * 512, 512), ic == 0, ic == 21,
                                      w=wd if ic == 0 else (),
                                      i=[("PE", 1)] if ic == 21 else ())
                        xsl = XR[:, o * 2048 + T * 512: o * 2048 + (T + 1) * 512]
                        if e == 0:
                            c_ve = vstep(lambda E, psD=psD, xsl=xsl: E.tensor_tensor(
                                xsl, xsl, psD[:, :], op=ALU.add), w=[("PE", c_pe)], i=[("VE", 1)])
                        else:
                            TMP = pf(PWK + 512, 512)
                            vstep(lambda E, psD=psD: E.tensor_tensor(
                                TMP, psD[:, :], pb(WB, 512), op=ALU.mult), w=[("PE", c_pe)])
                            c_ve = vstep(lambda E, xsl=xsl: E.tensor_tensor(
                                xsl, xsl, TMP, op=ALU.add), i=[("VE", 1)])
                        c_dve[o % 2] = c_ve
                        c_dpe[o % 2] = c_pe
                        if nexto[0] < 8:
                            on = nexto[0]
                            nexto[0] += 1
                            c_dwl[on] = dwl(dst3(ds[on % 2], 22, 128),
                                            src3(dsrc[:, on * 128:(on + 1) * 128]),
                                            w=[("PE", c_pe)])
            barrier()

        def phase_attn_b():
            ln_phase("plain")
            barrier()
            qmax = [0]
            kmax = [0]
            proj_fm(di["wq"], 0, 8, post_plain(OQ, BBQ))
            st_ms = [0, 0]
            cellc = [0]
            for o in range(8):
                for T in range(4):
                    SQ = pb(OWS + (T % 2) * 512, 512)
                    c_sq = vstep(lambda E, o=o, T=T, SQ=SQ: E.tensor_tensor(
                        SQ, pb(OQ + o * 2048 + T * 512, 512), pb(OQ + o * 2048 + T * 512, 512),
                        op=ALU.mult), i=[("VE", 1)])
                    psROW = PS[4 + T % 2]
                    c_pe3 = mm(psROW[0:1, :], ONES[:, 0:1], SQ, True, True,
                               w=[("VE", c_sq), ("VE", st_ms[T % 2])], i=[("PE", 1)])
                    cell = cellc[0]
                    cellc[0] += 1
                    c_red = vstep(lambda E, cell=cell, psROW=psROW: E.tensor_reduce(
                        pf(PMXQ + cell, 1, 0, 1), psROW[0:1, :], axis=AX.X, op=ALU.max),
                        w=[("PE", c_pe3)], i=[("VE", 1)])
                    st_ms[T % 2] = c_red
            finalize_max(PMXQ, cellc[0], PMXQ + 38)
            c_glo = 0
            for c in range(8):
                c_glo = vstep(lambda E, c=c: E.tensor_reduce(
                    pf(PGLO + c, 1), pb(OQ + c * 2048, 2048), axis=AX.X, op=ALU.add),
                    i=[("VE", 1)])
            c_pe = tr(PSM[0:8, 128:256], pf(PGLO, 8), IDF[:, :], w=[("VE", c_glo)], i=[("PE", 1)])
            c_ve = vstep(lambda E: E.tensor_copy(V8[0:8, :], PSM[0:8, 128:256]),
                         w=[("PE", c_pe)], i=[("VE", 1)])
            dwl(cc_k_in[1024:1025, 0:1024].rearrange("a (b c) -> (a b) c", b=8), V8[0:8, :],
                w=[("VE", c_ve)])
            proj_fm(di["wk"], 0, 8, post_stage_bias(BBK, (cc_k_in, 0), PMXK, kmax))
            proj_v_bias(di["wv"], 0)
            c_ve = finalize_max(PMXK, kmax[0], PMXK + 38)
            dwl(cc_k_in[1025:1026, 0:1], pf(PMXK + 38, 1, 0, 1), w=[("VE", c_ve)])
            gstep(lambda E: E.collective_compute(
                "AllGather", ALU.bypass, replica_groups=CCG,
                ins=[cc_k_in.ap()], outs=[cc_k_out.ap()]),
                w=[("IO", P.val("IO")), ("WL", P.val("WL"))], i=[("CC", 1)])
            gstep(lambda E: E.collective_compute(
                "AllGather", ALU.bypass, replica_groups=CCG,
                ins=[ccvin.ap()], outs=[ccvout.ap()]), i=[("CC", 1)])
            proj_fm(di["wfq"], 0, 8, post_plain(OH), rhs_off=OQ)
            barrier()
            dwl(pf(PGLA, 128, 0, 8), cc_k_out[1024:1025, 0:1024].rearrange("a (b c) -> (a b) c", b=8),
                w=[("CC", P.val("CC"))])
            c_wl = dwl(pf(PGLB, 128, 0, 8), cc_k_out[2050:2051, 0:1024].rearrange("a (b c) -> (a b) c", b=8))
            c = vstep(lambda E: E.tensor_tensor(pf(PGLA, 128, 0, 8), pf(PGLA, 128, 0, 8),
                                                pf(PGLB, 128, 0, 8), op=ALU.add),
                      w=[("WL", c_wl)], i=[("VE", 1)])
            c = vstep(lambda E: E.tensor_scalar(pf(PGLA, 128, 0, 8), pf(PGLA, 128, 0, 8),
                                                1.0 / 4096.0, None, op0=ALU.mult),
                      w=[("VE", c)], i=[("VE", 1)])
            c_pe = tr(PSM[:, 32:40], pf(PGLA, 128, 0, 8), IDF[0:8, 0:8], w=[("VE", c)], i=[("PE", 1)])
            c = vstep(lambda E: E.tensor_copy(pf(PGLC, 8), PSM[:, 32:40]),
                      w=[("PE", c_pe)], i=[("VE", 1)])
            c = vstep(lambda E: E.tensor_copy(pb(OWS + 3072, 8), pf(PGLC, 8)),
                      w=[("VE", c)], i=[("VE", 1)])
            dwl(pf(PMXK + 36, 1, 0, 1), cc_k_out[1025:1026, 0:1])
            c_wl = dwl(pf(PMXK + 37, 1, 0, 1), cc_k_out[2051:2052, 0:1])
            make_negB(PMXQ + 38, [PMXK + 36, PMXK + 37], w=[("WL", c_wl), ("VE", c)])
            slots = [OWA, OWA + 1024]
            slot_free = [(), ()]
            for o in range(8):
                c_wl = dwl(dst3(slots[o % 2], 8, 128),
                           src3(di["wfg"][:, o * 128:(o + 1) * 128]),
                           w=slot_free[o % 2])
                for kc in range(8):
                    c_pe = mm(PSM[:, 40:41], pb(slots[o % 2] + kc * 128, 128),
                              pb(OWS + 3072 + kc, 1), kc == 0, kc == 7,
                              w=[("WL", c_wl), ("VE", P.val("VE"))] if kc == 0 else (),
                              i=[("PE", 1)] if kc == 7 else ())
                c_ve = vstep(lambda E, o=o: E.tensor_scalar(
                    BANKS[:, BGB + o:BGB + o + 1], PSM[:, 40:41],
                    BANKS[:, BBFQ + o:BBFQ + o + 1], BANKS[:, BBFG + o:BBFG + o + 1],
                    op0=ALU.add, op1=ALU.add), w=[("PE", c_pe)], i=[("VE", 1)])
                slot_free[o % 2] = [("PE", c_pe)]
            barrier()
            CS = [OWS + 1024, OWS + 1536]
            for T in range(4):
                for o in range(8):
                    c_ac = astep(lambda E, o=o, T=T, s=CS[o % 2]: E.activation(
                        pb(s, 512), pb(OH + o * 2048 + T * 512, 512), AF.Tanh,
                        bias=BANKS[:, BGB + o:BGB + o + 1]),
                        w=[("PE", P.val("PE"))] if o >= 2 else (), i=[("AC", 1)])
                    c_pe = mm(PS[4][0:32, :], UBt[:, o * 32:(o + 1) * 32], pb(CS[o % 2], 512),
                              o == 0, o == 7, w=[("AC", c_ac)], i=[("PE", 1)])
                c_ac = astep(lambda E, T=T: E.activation(
                    pf(PPZ + T * 512, 512, 0, 32), PS[4][0:32, :], AF.Sigmoid),
                    w=[("PE", c_pe)], i=[("AC", 1)])
                vstep(lambda E, T=T: E.tensor_scalar(
                    pf(PPZ + T * 512, 512, 0, 32), pf(PPZ + T * 512, 512, 0, 32), 4096.0,
                    None, op0=ALU.mult), w=[("AC", c_ac)], i=[("VE", 1)])
            barrier()
            for qt in range(16):
                c_pe = tr(PSM[:, 64:96], pf(PPZ + qt * 128, 128, 0, 32), IDF[0:32, 0:32],
                          w=[("VE", P.val("VE"))], i=[("PE", 1)])
                c = vstep(lambda E: E.reciprocal(pf(PRC, 16), PSM[:, 80:96]),
                          w=[("PE", c_pe)], i=[("VE", 1)])
                c = vstep(lambda E, qt=qt: E.tensor_scalar(
                    pf(PUW + qt * 32, 16), pf(PRC, 16), 4.0 * SQ8, None, op0=ALU.mult),
                    w=[("VE", c)], i=[("VE", 1)])
                c = vstep(lambda E, qt=qt: E.tensor_tensor(
                    pf(PUW + qt * 32 + 16, 16), pf(PUW + qt * 32, 16), PSM[:, 64:80],
                    op=ALU.mult), w=[("VE", c)], i=[("VE", 1)])
            dio(PFt[:, 0:4096], di["Jrow"][:, :])
            barrier()
            attn_core(False)
            barrier()
            proj_fm(di["wo"], 0, 8, post_accum_x(BBO))
            barrier()

        phase_attn_a(lv=min(nphases, 4))
        if nphases >= 5:
            phase_ffn()
        if nphases >= 6:
            phase_attn_b()
        if nphases >= 7:
            phase_ffn()

        for c in range(8):
            dio(y[c * 128:(c + 1) * 128, :], XR[:, c * 2048:(c + 1) * 2048],
                w=[("VE", P.val("VE"))] if c == 0 else ())
        for (tname, pool, p0, np_, c0, ncols) in debug_taps:
            src = {"PB": PBt, "PF": PFt, "XR": XR, "CONSTB": CONSTB}[pool]
            if pool in ("PB", "CONSTB"):
                dwl(taps[tname][:, :], src[p0:p0 + np_, c0:c0 + ncols])
            else:
                dio(taps[tname][:, :], src[p0:p0 + np_, c0:c0 + ncols])
        P.step("sync", None, [("IO", P.val("IO")), ("WL", P.val("WL"))])

        with nc.Block() as blk:
            @blk.sync
            def _(E):
                for f in P.steps["sync"]:
                    f(E)

            @blk.scalar
            def _(E):
                for f in P.steps["scalar"]:
                    f(E)

            @blk.vector
            def _(E):
                for f in P.steps["vector"]:
                    f(E)

            @blk.tensor
            def _(E):
                for f in P.steps["tensor"]:
                    f(E)

            @blk.gpsimd
            def _(E):
                for f in P.steps["gpsimd"]:
                    f(E)
    nc._plan = P
    return nc, di, taps


def kernel(**inputs):
    cores = make_core_inputs(inputs)
    nc, di, taps = build_nc()
    res = run_bass_kernel_spmd(nc, cores, core_ids=list(range(8)))
    out = np.empty((4, 4096, 1024), np.float32)
    for cid in range(8):
        b, h = cid // 2, cid % 2
        out[b, h * 2048:(h + 1) * 2048, :] = np.asarray(res.results[cid]["y"]).T
    return out


# version 05 (final)
# speedup vs baseline: 1.2552x; 1.1784x over previous
import numpy as np
import ml_dtypes
from contextlib import ExitStack

import concourse.bass as bass
import concourse.mybir as mybir
from concourse.bass_utils import run_bass_kernel_spmd

F32 = mybir.dt.float32
BF16 = mybir.dt.bfloat16
AF = mybir.ActivationFunctionType
ALU = mybir.AluOpType
AX = mybir.AxisListType
BF = ml_dtypes.bfloat16

EPS32 = float(np.finfo(np.float32).eps)
SQ8 = float(np.sqrt(8.0))

PB_N = 53248
PF_N = 8192
OH = 0
OQ = 16384
OKV = 32768
OWA = 40960
OWS = 49152
OLNR = OQ + 15360
PJ = 0
PWK = 4096
PPZ = 5120
PROW0 = 7168
PROW1 = 7680
PL0 = 4096
PL1 = 4112
PGLO = 4128
PMXS = 4140
PGLA = 4160
PGLB = 4288
PGLC = 4416
PMXQ = 4456
PMXK = 4500
PRC = 4544
PTMP = 4560
PCEPS1 = 4592
PCEPS2 = 4593
PUW = 4608
BNW, BBQ, BBK, BBO, BBFQ, BBFG, BGB = 0, 8, 16, 24, 32, 40, 48
BNEGB, BLI0, BLI1, BLS0, BLS1, BDEN = 56, 57, 58, 59, 60, 61

CCG = [[0, 1], [2, 3], [4, 5], [6, 7]]
ENGINES = ("sync", "scalar", "vector", "tensor", "gpsimd")


def n_live(qt):
    return min(8, (qt * 128 + 2048 + 132 + 511) // 512)


class Plan:
    def __init__(self):
        self.steps = {e: [] for e in ENGINES}
        self.cnt = {s: 0 for s in ("WL", "IO", "PE", "VE", "AC", "PO", "CC")}
        self.sems = {}
        self.log = []

    def val(self, name):
        return self.cnt[name]

    def step(self, eng, emitfn, w=(), i=()):
        for s, v in w:
            assert v <= self.cnt[s], f"[{eng}] wait {s}>={v} but only {self.cnt[s]} planned"
        last = None
        for s, a in i:
            self.cnt[s] += a
            last = self.cnt[s]
        self.log.append((eng, tuple(w), tuple((s, self.cnt[s]) for s, a in i)))
        sems = self.sems
        w = tuple(w)
        i = tuple(i)

        def run(E, w=w, i=i, emitfn=emitfn):
            for s, v in w:
                E.wait_ge(sems[s], v)
            inst = None
            if emitfn is not None:
                inst = emitfn(E)
            for s, a in i:
                assert inst is not None, "inc without instruction"
                inst = inst.then_inc(sems[s], a)
        self.steps[eng].append(run)
        return last


def _bcol(v):
    return np.ascontiguousarray(np.asarray(v, np.float32).reshape(8, 128))


def make_core_inputs(inputs):
    f = np.float32
    sh = {}
    sh["wqkvT"] = np.ascontiguousarray(np.asarray(inputs["L0_attn_a_w_qkv"], f).T)
    sh["woutT"] = np.ascontiguousarray(np.asarray(inputs["L0_attn_a_w_out"], f).T)
    sh["nw8"] = _bcol(inputs["L0_attn_a_norm_weight"])
    pm = np.asarray(inputs["L0_attn_a_persist_mem"], f)
    pmk = np.zeros((128, 32), f)
    pmv = np.zeros((4, 1024), f)
    for hpi in range(8):
        for hd in range(2):
            h_ = 2 * hpi + hd
            pmk[hd * 64:(hd + 1) * 64, hpi * 4:hpi * 4 + 4] = pm[h_].T
            pmv[:, hpi * 128 + hd * 64: hpi * 128 + (hd + 1) * 64] = pm[h_]
    sh["pmk"] = pmk.astype(BF)
    sh["pmv"] = pmv.astype(BF)
    for k in ("wq", "wk", "wv", "wo", "wfq", "wfg"):
        sh[k] = np.ascontiguousarray(np.asarray(inputs[f"L0_attn_b_{k}"], f))
    for k in ("bq", "bk", "bo", "bfq", "bfg"):
        sh[k + "8"] = _bcol(inputs[f"L0_attn_b_{k}"])
    sh["bv_row"] = np.asarray(inputs["L0_attn_b_bv"], f).reshape(1, 1024).astype(BF)
    up = np.asarray(inputs["L0_attn_b_up"], f)
    uz = np.asarray(inputs["L0_attn_b_uz"], f)
    U = np.zeros((8, 128, 32), f)
    for c in range(8):
        for r in range(2):
            h_ = 2 * c + r
            U[c, r * 64:(r + 1) * 64, h_] = up[h_, 0]
            U[c, r * 64:(r + 1) * 64, 16 + h_] = uz[h_, 0]
    sh["U"] = np.ascontiguousarray(U.reshape(1024, 32)).astype(BF)
    sh["sgT"] = np.ascontiguousarray(np.asarray(inputs["L0_ffn_sg"], f).T)
    sh["suT"] = np.ascontiguousarray(np.asarray(inputs["L0_ffn_su"], f).T)
    sh["sdT"] = np.ascontiguousarray(np.asarray(inputs["L0_ffn_sd"], f).T)
    sh["rgT"] = np.ascontiguousarray(np.transpose(np.asarray(inputs["L0_ffn_rg"], f), (0, 2, 1)))
    sh["ruT"] = np.ascontiguousarray(np.transpose(np.asarray(inputs["L0_ffn_ru"], f), (0, 2, 1)))
    sh["rdT"] = np.ascontiguousarray(np.transpose(np.asarray(inputs["L0_ffn_rd"], f), (0, 2, 1)))
    sh["wrT"] = np.ascontiguousarray(np.asarray(inputs["L0_ffn_wr"], f).T)
    sh["rb_row"] = np.asarray(inputs["L0_ffn_rb"], f).reshape(1, 7).astype(BF)

    sh["identb"] = np.eye(128).astype(BF)
    sh["identf"] = np.eye(128, dtype=f)
    sh["negidentb"] = (-np.eye(128)).astype(BF)
    perm = np.zeros((128, 128), f)
    for i in range(64):
        perm[2 * i + 1, 2 * i] = 1.0
        perm[2 * i, 2 * i + 1] = 1.0
    sh["permb"] = perm.astype(BF)
    sh["onesb"] = np.ones((128, 128), BF)
    canon = np.zeros((128, 6 * 512), f)
    rr = np.arange(128)[:, None]
    cc = np.arange(512)[None, :]
    for t in range(4):
        canon[:, (1 + t) * 512:(2 + t) * 512] = np.where(cc <= t * 128 + rr, 0.0, -1.0e9)
    canon[:, 5 * 512:] = -1.0e9
    sh["canon"] = canon.astype(BF)
    sh["Jrow"] = np.broadcast_to(np.arange(4096, dtype=f), (128, 4096)).copy()

    cores = []
    x = np.asarray(inputs["L0_attn_a_x"], f)
    inv_freq = 1.0 / (10000.0 ** (np.arange(0, 64, 2, dtype=np.float64) / 64))
    for cid in range(8):
        b, h = cid // 2, cid % 2
        d = dict(sh)
        d["xT"] = np.ascontiguousarray(x[b, h * 2048:(h + 1) * 2048, :].T)
        pos = np.arange(h * 2048, (h + 1) * 2048, dtype=np.float64)
        fr = np.repeat(pos[None, :] * inv_freq[:, None], 2, axis=0)
        cos = np.cos(fr)
        sgn = np.where((np.arange(64) % 2) == 0, -1.0, 1.0)[:, None]
        sin = np.sin(fr) * sgn
        d["cosT"] = np.ascontiguousarray(np.tile(cos, (2, 1)).astype(f))
        d["sinT"] = np.ascontiguousarray(np.tile(sin, (2, 1)).astype(f))
        canon6 = canon.astype(BF).reshape(128, 6, 512).transpose(1, 0, 2)
        th = 4 * h
        sel = np.zeros((16, 8), np.int64)
        for qt in range(16):
            t_ = th + qt // 4
            for ci in range(8):
                sel[qt, ci] = 0 if ci < t_ else ((1 + qt % 4) if ci == t_ else 5)
        d["masksel"] = np.ascontiguousarray(
            canon6[sel.reshape(-1)].reshape(16 * 8 * 128, 512))
        cores.append(d)
    return cores


def build_nc(debug_taps=(), nphases=8, core_lim=(8, 16), core_stage=4):
    nc = bass.Bass("TRN2", target_bir_lowering=False)
    di = {}

    def inp(name, shape, dt=F32):
        di[name] = nc.dram_tensor(name, list(shape), dt, kind="ExternalInput")
        return di[name]

    inp("xT", (1024, 2048))
    inp("wqkvT", (1024, 3072)); inp("woutT", (1024, 1024)); inp("nw8", (8, 128))
    inp("pmk", (128, 32), BF16); inp("pmv", (4, 1024), BF16)
    for k in ("wq", "wk", "wv", "wo", "wfq", "wfg"):
        inp(k, (1024, 1024))
    for k in ("bq8", "bk8", "bo8", "bfq8", "bfg8"):
        inp(k, (8, 128))
    inp("bv_row", (1, 1024), BF16); inp("U", (1024, 32), BF16)
    inp("sgT", (1024, 2816)); inp("suT", (1024, 2816)); inp("sdT", (2816, 1024))
    inp("rgT", (7, 1024, 2816)); inp("ruT", (7, 1024, 2816)); inp("rdT", (7, 2816, 1024))
    inp("wrT", (1024, 7)); inp("rb_row", (1, 7), BF16)
    inp("identb", (128, 128), BF16); inp("identf", (128, 128)); inp("negidentb", (128, 128), BF16)
    inp("permb", (128, 128), BF16); inp("onesb", (128, 128), BF16)
    inp("canon", (128, 6 * 512), BF16); inp("Jrow", (128, 4096))
    inp("masksel", (16384, 512), BF16)
    inp("cosT", (128, 2048)); inp("sinT", (128, 2048))

    y = nc.dram_tensor("y", [1024, 2048], F32, kind="ExternalOutput")
    cc_k_in = nc.dram_tensor("cc_k_in", [1026, 2048], BF16)
    cc_k_out = nc.dram_tensor("cc_k_out", [2052, 2048], BF16)
    ccvin = nc.dram_tensor("ccvin", [2048, 1024], BF16)
    ccvout = nc.dram_tensor("ccvout", [4096, 1024], BF16)

    taps = {}
    for (tname, pool, p0, np_, c0, ncols) in debug_taps:
        taps[tname] = nc.dram_tensor("tap_" + tname, [np_, ncols], F32, kind="ExternalOutput")

    P = Plan()
    REGS = {}
    with ExitStack() as ST:
        E_ = ST.enter_context
        PBt = E_(nc.sbuf_tensor("PB", [128, PB_N], BF16))
        PFt = E_(nc.sbuf_tensor("PF", [128, PF_N], F32))
        XR = E_(nc.sbuf_tensor("XR", [128, 16384], F32))
        CONSTB = E_(nc.sbuf_tensor("CONSTB", [128, 512], BF16))
        IDF = E_(nc.sbuf_tensor("IDF", [128, 128], F32))
        PMKt = E_(nc.sbuf_tensor("PMK", [128, 32], BF16))
        PMVt = E_(nc.sbuf_tensor("PMV", [4, 1024], BF16))
        UBt = E_(nc.sbuf_tensor("UB", [128, 256], BF16))
        BANKS = E_(nc.sbuf_tensor("BANKS", [128, 64], F32))
        ROWSB = E_(nc.sbuf_tensor("ROWSB", [1, 1036], BF16))
        V8 = E_(nc.sbuf_tensor("V8", [8, 128], F32))
        PS = [E_(nc.psum_tensor(f"PS{i}", [128, 512], F32)) for i in range(6)]
        PST = E_(nc.psum_tensor("PST", [128, 1024], BF16))
        PSM = E_(nc.psum_tensor("PSM", [128, 256], F32))

        for s in ("WL", "IO", "PE", "VE", "AC", "PO", "CC"):
            P.sems[s] = E_(nc.semaphore(s))

        IDB = CONSTB[:, 0:128]
        NIDB = CONSTB[:, 128:256]
        PERM = CONSTB[:, 256:384]
        ONES = CONSTB[:, 384:512]

        def pb(c0, n, p0=0, np_=128):
            return PBt[p0:p0 + np_, c0:c0 + n]

        def pf(c0, n, p0=0, np_=128):
            return PFt[p0:p0 + np_, c0:c0 + n]

        def src3(ap):
            return ap.rearrange("(k p) o -> p k o", p=128)

        def dst3(col, nk, ncols):
            return pb(col, nk * ncols).rearrange("p (k o) -> p k o", k=nk)

        def mm(dst, lh, rh, start, stop, w=(), i=()):
            return P.step("tensor", lambda E, a=dst, b=lh, c=rh, s=start, t=stop:
                          E.matmul(a, b, c, start=s, stop=t), w, i)

        def tr(dst, src, ident, w=(), i=()):
            return P.step("tensor", lambda E, a=dst, b=src, c=ident: E.transpose(a, b, c), w, i)

        def trg(dst, src, ident, start, stop, w=(), i=()):
            return P.step("tensor", lambda E, a=dst, b=src, c=ident, s=start, t=stop:
                          E.matmul(a, b, c, start=s, stop=t, is_transpose=True), w, i)

        def vstep(fn, w=(), i=()):
            return P.step("vector", fn, w, i)

        def astep(fn, w=(), i=()):
            return P.step("scalar", fn, w, i)

        def gstep(fn, w=(), i=()):
            return P.step("gpsimd", fn, w, i)

        def dio(out, in_, w=(), **kw):
            return P.step("sync", lambda E, a=out, b=in_, kw=kw: E.dma_start(out=a, in_=b, **kw),
                          w, [("IO", 16)])

        def dwl(out, in_, w=(), **kw):
            return P.step("gpsimd", lambda E, a=out, b=in_, kw=kw: E.dma_start(out=a, in_=b, **kw),
                          w, [("WL", 16)])

        def barrier():
            ws = [(s, P.val(s)) for s in ("PE", "VE", "AC", "PO", "WL", "IO", "CC")]
            for e in ENGINES:
                P.step(e, None, ws)

        dio(XR[:, :].rearrange("p (a t) -> p a t", a=8), di["xT"][:, :].rearrange("(a p) t -> p a t", p=128))
        dio(CONSTB[:, 0:128], di["identb"][:, :])
        dio(CONSTB[:, 128:256], di["negidentb"][:, :])
        dio(CONSTB[:, 256:384], di["permb"][:, :])
        dio(CONSTB[:, 384:512], di["onesb"][:, :])
        dio(IDF[:, :], di["identf"][:, :])
        dio(PMKt[:, :], di["pmk"][:, :])
        dio(PMVt[:, :], di["pmv"][:, :])
        for c in range(8):
            dio(UBt[:, c * 32:(c + 1) * 32], di["U"][c * 128:(c + 1) * 128, :])
        dio(ROWSB[0:1, 0:1024], di["bv_row"][:, :])
        dio(ROWSB[0:1, 1024:1031], di["rb_row"][:, :])
        dio(PFt[:, 0:2048], di["cosT"][:, :])
        c_setup = dio(PFt[:, 2048:4096], di["sinT"][:, :])
        vstep(lambda E: E.memset(pf(PCEPS1, 1, 0, 1), EPS32 * 1e-5))
        vstep(lambda E: E.memset(pf(PCEPS2, 1, 0, 1), 1e-5), i=[("VE", 1)])
        for vn, bc in (("nw8", BNW), ("bq8", BBQ), ("bk8", BBK), ("bo8", BBO),
                       ("bfq8", BBFQ), ("bfg8", BBFG)):
            c_io = dio(V8[:, :], di[vn][:, :], w=[("VE", P.val("VE"))])
            c_pe = tr(PSM[:, 0:8], V8[0:8, 0:128], IDF[0:8, 0:8], w=[("IO", c_io)], i=[("PE", 1)])
            vstep(lambda E, bc=bc: E.tensor_copy(BANKS[:, bc:bc + 8], PSM[:, 0:8]),
                  w=[("PE", c_pe)], i=[("VE", 1)])
        barrier()

        def ln_phase(which):
            xb = [OWS + 0, OWS + 512]
            xq = [OWS + 1024, OWS + 1536]
            for T in range(4):
                t0 = T * 512
                pe_done = []
                for c in range(8):
                    w = [("PE", pe_done[-2])] if len(pe_done) >= 2 else \
                        ([("PE", P.val("PE"))] if (T > 0 and c == 0) else [])
                    vstep(lambda E, c=c, t0=t0, s=xb[c % 2]: E.tensor_copy(
                        pb(s, 512), XR[:, c * 2048 + t0: c * 2048 + t0 + 512]), w=w)
                    c_ve = vstep(lambda E, c=c, t0=t0, s=xq[c % 2]: E.tensor_tensor(
                        pb(s, 512), XR[:, c * 2048 + t0: c * 2048 + t0 + 512],
                        XR[:, c * 2048 + t0: c * 2048 + t0 + 512], op=ALU.mult), i=[("VE", 1)])
                    mm(PS[0][0:1, :], ONES[:, 0:1], pb(xb[c % 2], 512), c == 0, c == 7,
                       w=[("VE", c_ve)])
                    c_pe = mm(PS[1][0:1, :], ONES[:, 0:1], pb(xq[c % 2], 512), c == 0, c == 7,
                              i=[("PE", 1)])
                    pe_done.append(c_pe)
                RM = pf(PROW0, 512, 0, 1)
                RV = pf(PROW1, 512, 0, 1)
                RT = pf(PWK, 512, 0, 1)
                vstep(lambda E: E.tensor_scalar(RM, PS[0][0:1, :], 1.0 / 1024.0, None, op0=ALU.mult),
                      w=[("PE", pe_done[-1])])
                vstep(lambda E: E.tensor_scalar(RV, PS[1][0:1, :], 1.0 / 1024.0, None, op0=ALU.mult))
                vstep(lambda E: E.tensor_tensor(RT, RM, RM, op=ALU.mult))
                c_ve = vstep(lambda E: E.tensor_tensor(RV, RV, RT, op=ALU.subtract), i=[("VE", 1)])
                lnsc = (1.0 + EPS32) if which == "rms" else 1.0
                lnbc = PCEPS1 if which == "rms" else PCEPS2
                astep(lambda E, s=lnsc, b=lnbc: E.activation(RT, RV, AF.Ln,
                                                             bias=pf(b, 1, 0, 1), scale=s),
                      w=[("VE", c_ve)])
                c_ac = astep(lambda E: E.activation(RV, RT, AF.Exp, scale=-0.5), i=[("AC", 1)])
                c_ve = vstep(lambda E: E.scalar_tensor_tensor(RT, RM, -1.0, RV,
                                                              op0=ALU.mult, op1=ALU.mult),
                             w=[("AC", c_ac)], i=[("VE", 1)])
                R1B = pb(OLNR, 512, 0, 1)
                R0B = pb(OLNR + 512, 512, 0, 1)
                vstep(lambda E: E.tensor_copy(R1B, RV), w=[("VE", c_ve)])
                c_ve = vstep(lambda E: E.tensor_copy(R0B, RT), i=[("VE", 1)])
                mm(PS[4][:, :], ONES[0:1, :], R1B, True, True,
                   w=[("VE", c_ve), ("VE", P.val("VE"))])
                c_pe = mm(PS[5][:, :], ONES[0:1, :], R0B, True, True, i=[("PE", 1)])
                TT = pf(PWK, 512)
                UT = pf(PWK + 512, 512)
                for c in range(8):
                    w = [("PE", c_pe)] if c == 0 else []
                    vstep(lambda E, c=c, t0=t0: E.tensor_tensor(
                        TT, XR[:, c * 2048 + t0: c * 2048 + t0 + 512], PS[4][:, :], op=ALU.mult), w=w)
                    if which == "rms":
                        vstep(lambda E: E.tensor_tensor(UT, TT, PS[5][:, :], op=ALU.add))
                        vstep(lambda E, c=c, t0=t0: E.tensor_scalar(
                            pb(OH + c * 2048 + t0, 512), UT, BANKS[:, BNW + c:BNW + c + 1],
                            None, op0=ALU.mult), i=[("VE", 1)])
                    else:
                        vstep(lambda E, c=c, t0=t0: E.tensor_tensor(
                            pb(OH + c * 2048 + t0, 512), TT, PS[5][:, :], op=ALU.add), i=[("VE", 1)])

        def proj_fm(wsrc, wcol0, nco, post, rhs_off=OH, slotA=OWA, slotB=OWA + 1024):
            slots = [slotA, slotB]
            slot_free = [(), ()]
            ps_free = {0: 0, 1: 0}
            last_ve = 0
            for o in range(nco):
                sl = slots[o % 2]
                c_wl = dwl(dst3(sl, 8, 128),
                           src3(wsrc[:, wcol0 + o * 128: wcol0 + (o + 1) * 128]),
                           w=slot_free[o % 2])
                pes = []
                for T in range(4):
                    psS = PS[T % 2]
                    w0 = [("WL", c_wl), ("VE", ps_free[T % 2])]
                    for kc in range(8):
                        c_pe = mm(psS[:, :], pb(sl + kc * 128, 128),
                                  pb(rhs_off + kc * 2048 + T * 512, 512),
                                  kc == 0, kc == 7,
                                  w=w0 if kc == 0 else (),
                                  i=[("PE", 1)] if kc == 7 else ())
                    pes.append(c_pe)
                    c_psfree, last_ve = post(o, T, psS, c_pe)
                    ps_free[T % 2] = c_psfree
                slot_free[o % 2] = [("PE", pes[-1])]
            return last_ve

        def post_plain(dst_off, bias_bank=None):
            def post(o, T, psS, c_pe):
                if bias_bank is not None:
                    c_ve = vstep(lambda E, o=o, T=T, psS=psS: E.tensor_scalar(
                        pb(dst_off + o * 2048 + T * 512, 512), psS[:, :],
                        BANKS[:, bias_bank + o:bias_bank + o + 1], None, op0=ALU.add),
                        w=[("PE", c_pe)], i=[("VE", 1)])
                else:
                    c_ve = vstep(lambda E, o=o, T=T, psS=psS: E.tensor_copy(
                        pb(dst_off + o * 2048 + T * 512, 512), psS[:, :]),
                        w=[("PE", c_pe)], i=[("VE", 1)])
                return c_ve, c_ve
            return post

        def post_accum_x(bias_bank=None):
            def post(o, T, psS, c_pe):
                xsl = XR[:, o * 2048 + T * 512: o * 2048 + (T + 1) * 512]
                if bias_bank is not None:
                    c_ve = vstep(lambda E, o=o, psS=psS, xsl=xsl: E.scalar_tensor_tensor(
                        xsl, psS[:, :], BANKS[:, bias_bank + o:bias_bank + o + 1], xsl,
                        op0=ALU.add, op1=ALU.add), w=[("PE", c_pe)], i=[("VE", 1)])
                else:
                    c_ve = vstep(lambda E, psS=psS, xsl=xsl: E.tensor_tensor(
                        xsl, xsl, psS[:, :], op=ALU.add), w=[("PE", c_pe)], i=[("VE", 1)])
                return c_ve, c_ve
            return post

        def mk_post_rope(dst_off, stage_rows=None, maxbase=None, maxcells=None):
            st = {"psR": [0, 0], "stg": [0, 0], "msr": [0, 0], "cell": 0}

            def post(o, T, psS, c_pe):
                kraw = OWS + 2048 + (T % 2) * 512
                c_ve1 = vstep(lambda E, kraw=kraw, psS=psS: E.tensor_copy(pb(kraw, 512), psS[:, :]),
                              w=[("PE", c_pe)], i=[("VE", 1)])
                psR = PS[2 + T % 2]
                c_pe2 = mm(psR[:, :], PERM, pb(kraw, 512), True, True,
                           w=[("VE", max(c_ve1, st["psR"][T % 2]))], i=[("PE", 1)])
                TT = pf(PWK, 512)
                UT = pf(PWK + 512, 512)
                c_rot = vstep(lambda E, T=T, psR=psR: E.tensor_tensor(
                    TT, psR[:, :], PFt[:, 2048 + T * 512: 2048 + (T + 1) * 512], op=ALU.mult),
                    w=[("PE", c_pe2)], i=[("VE", 1)])
                st["psR"][T % 2] = c_rot
                vstep(lambda E, kraw=kraw, T=T: E.tensor_tensor(
                    UT, pb(kraw, 512), PFt[:, T * 512:(T + 1) * 512], op=ALU.mult))
                if dst_off is not None:
                    dst = dst_off + o * 2048 + T * 512
                else:
                    dst = OWS + 3072 + (T % 2) * 512
                wext = [("IO", st["stg"][T % 2])] if (stage_rows is not None and dst_off is None) else []
                c_ve = vstep(lambda E, dst=dst: E.tensor_tensor(pb(dst, 512), UT, TT, op=ALU.add),
                             w=wext, i=[("VE", 1)])
                if stage_rows is not None:
                    dram, rowbase = stage_rows
                    c_io = dio(dram[rowbase + o * 128: rowbase + (o + 1) * 128,
                                    T * 512:(T + 1) * 512], pb(dst, 512), w=[("VE", c_ve)])
                    st["stg"][T % 2] = c_io
                if maxbase is not None:
                    SQ = pb(OWS + (T % 2) * 512, 512)
                    c_sq = vstep(lambda E, dst=dst, SQ=SQ: E.tensor_tensor(
                        SQ, pb(dst, 512), pb(dst, 512), op=ALU.mult), i=[("VE", 1)])
                    psROW = PS[4 + T % 2]
                    c_pe3 = mm(psROW[0:1, :], ONES[:, 0:1], SQ,
                               True, True, w=[("VE", max(c_sq, st["msr"][T % 2]))], i=[("PE", 1)])
                    cell = st["cell"]
                    st["cell"] += 1
                    c_red = vstep(lambda E, cell=cell, psROW=psROW: E.tensor_reduce(
                        pf(maxbase + cell, 1, 0, 1), psROW[0:1, :], axis=AX.X, op=ALU.max),
                        w=[("PE", c_pe3)], i=[("VE", 1)])
                    st["msr"][T % 2] = c_red
                    maxcells[0] = max(maxcells[0], st["cell"])
                return c_ve1, c_ve
            return post

        def post_stage_bias(bias_bank, stage_rows, maxbase=None, maxcells=None):
            st = {"stg": [0, 0], "msr": [0, 0], "cell": 0}

            def post(o, T, psS, c_pe):
                dst = OWS + 3072 + (T % 2) * 512
                c_ve = vstep(lambda E, o=o, dst=dst, psS=psS: E.tensor_scalar(
                    pb(dst, 512), psS[:, :], BANKS[:, bias_bank + o: bias_bank + o + 1],
                    None, op0=ALU.add), w=[("PE", c_pe), ("IO", st["stg"][T % 2])], i=[("VE", 1)])
                dram, rowbase = stage_rows
                c_io = dio(dram[rowbase + o * 128: rowbase + (o + 1) * 128,
                                T * 512:(T + 1) * 512], pb(dst, 512), w=[("VE", c_ve)])
                st["stg"][T % 2] = c_io
                if maxbase is not None:
                    SQ = pb(OWS + (T % 2) * 512, 512)
                    c_sq = vstep(lambda E, dst=dst, SQ=SQ: E.tensor_tensor(
                        SQ, pb(dst, 512), pb(dst, 512), op=ALU.mult), i=[("VE", 1)])
                    psROW = PS[4 + T % 2]
                    c_pe3 = mm(psROW[0:1, :], ONES[:, 0:1], SQ, True, True,
                               w=[("VE", max(c_sq, st["msr"][T % 2]))], i=[("PE", 1)])
                    cell = st["cell"]
                    st["cell"] += 1
                    c_red = vstep(lambda E, cell=cell, psROW=psROW: E.tensor_reduce(
                        pf(maxbase + cell, 1, 0, 1), psROW[0:1, :], axis=AX.X, op=ALU.max),
                        w=[("PE", c_pe3)], i=[("VE", 1)])
                    st["msr"][T % 2] = c_red
                    maxcells[0] = max(maxcells[0], st["cell"])
                return c_ve, c_ve
            return post

        def proj_v_nobias(wsrc, wcol0, vslot=OWA + 2048):
            c_prev_pe = P.val("PE")
            c_stage_io = [0, 0]
            for vo in range(2):
                c_wl = dwl(dst3(vslot, 8, 512),
                           src3(wsrc[:, wcol0 + vo * 512: wcol0 + (vo + 1) * 512]),
                           w=[("PE", c_prev_pe)])
                for t16 in range(16):
                    psS = PS[t16 % 2]
                    w0 = [("WL", c_wl), ("VE", P.val("VE") if t16 >= 2 else 0)]
                    for kc in range(8):
                        c_pe = mm(psS[:, :], pb(OH + kc * 2048 + t16 * 128, 128),
                                  pb(vslot + kc * 512, 512), kc == 0, kc == 7,
                                  w=w0 if kc == 0 else (), i=[("PE", 1)] if kc == 7 else ())
                    dst = OWS + 3072 + (t16 % 2) * 512
                    c_ve = vstep(lambda E, dst=dst, psS=psS: E.tensor_copy(pb(dst, 512), psS[:, :]),
                                 w=[("PE", c_pe), ("IO", c_stage_io[t16 % 2])], i=[("VE", 1)])
                    c_io = dio(ccvin[t16 * 128:(t16 + 1) * 128, vo * 512:(vo + 1) * 512],
                               pb(dst, 512), w=[("VE", c_ve)])
                    c_stage_io[t16 % 2] = c_io
                    c_prev_pe = c_pe

        def proj_v_bias(wsrc, wcol0, vslot=OWA + 2048):
            c_prev_pe = P.val("PE")
            c_stage_io = [0, 0]
            for vo in range(2):
                c_wl = dwl(dst3(vslot, 8, 512),
                           src3(wsrc[:, wcol0 + vo * 512: wcol0 + (vo + 1) * 512]),
                           w=[("PE", c_prev_pe)])
                for t16 in range(16):
                    psS = PS[t16 % 2]
                    w0 = [("WL", c_wl), ("VE", P.val("VE") if t16 >= 2 else 0)]
                    for kc in range(8):
                        mm(psS[:, :], pb(OH + kc * 2048 + t16 * 128, 128), pb(vslot + kc * 512, 512),
                           kc == 0, False, w=w0 if kc == 0 else ())
                    c_pe = mm(psS[:, :], ONES[0:1, :], ROWSB[0:1, vo * 512:(vo + 1) * 512],
                              False, True, i=[("PE", 1)])
                    dst = OWS + 3072 + (t16 % 2) * 512
                    c_ve = vstep(lambda E, dst=dst, psS=psS: E.tensor_copy(pb(dst, 512), psS[:, :]),
                                 w=[("PE", c_pe), ("IO", c_stage_io[t16 % 2])], i=[("VE", 1)])
                    c_io = dio(ccvin[t16 * 128:(t16 + 1) * 128, vo * 512:(vo + 1) * 512],
                               pb(dst, 512), w=[("VE", c_ve)])
                    c_stage_io[t16 % 2] = c_io
                    c_prev_pe = c_pe

        def finalize_max(maxbase, ncells, dstcell):
            c = vstep(lambda E, n=ncells: E.tensor_reduce(
                pf(dstcell, 1, 0, 1), pf(maxbase, ncells, 0, 1), axis=AX.X, op=ALU.max),
                i=[("VE", 1)])
            return c

        def make_negB(qcell, kcells, w=()):
            c = vstep(lambda E, k0=kcells[0]: E.tensor_copy(pf(PMXS, 1, 0, 1), pf(k0, 1, 0, 1)),
                      w=w, i=[("VE", 1)])
            for kc_ in kcells[1:]:
                c = vstep(lambda E, kc_=kc_: E.tensor_tensor(
                    pf(PMXS, 1, 0, 1), pf(PMXS, 1, 0, 1), pf(kc_, 1, 0, 1), op=ALU.max),
                    w=[("VE", c)], i=[("VE", 1)])
            c = vstep(lambda E: E.tensor_tensor(pf(PMXS, 1, 0, 1), pf(PMXS, 1, 0, 1),
                                                pf(qcell, 1, 0, 1), op=ALU.add),
                      w=[("VE", c)], i=[("VE", 1)])
            c = vstep(lambda E: E.tensor_scalar(pf(PMXS, 1, 0, 1), pf(PMXS, 1, 0, 1),
                                                -1.0 / 16.0, None, op0=ALU.mult),
                      w=[("VE", c)], i=[("VE", 1)])
            c = vstep(lambda E: E.tensor_copy(ROWSB[0:1, 1031:1032], pf(PMXS, 1, 0, 1)),
                      w=[("VE", c)], i=[("VE", 1)])
            c_pe = mm(PSM[:, 16:17], ONES[0:1, :], ROWSB[0:1, 1031:1032], True, True,
                      w=[("VE", c)], i=[("PE", 1)])
            c = vstep(lambda E: E.tensor_copy(BANKS[:, BNEGB:BNEGB + 1], PSM[:, 16:17]),
                      w=[("PE", c_pe)], i=[("VE", 1)])
            return c

        def attn_core(is_a):
            nhp, nqt = core_lim
            Pslot = [OWS + 0, OWS + 512, OWS + 1024, OWS + 1536]
            PTs = [OWS + 2048, OWS + 2560]
            MSK = [OWS + 3072, OWS + 3584]
            T2IN_B = [OWS + 3072, OWA]
            T2N_B = [OWS + 3584, OWA + 512]
            OSB = OWS + 0
            msk_io = [0, 0]
            msk_pe = [0, 0]
            st_ = {"kv_io": 0, "avpe": P.val("PE"), "ofve": P.val("VE"),
                   "t2ctr": 0, "t2pe2": [0, 0], "t2po2": [0, 0]}
            for hp in range(nhp):
                w_kv = [("CC", P.val("CC")), ("PE", st_["avpe"])]
                dio(pb(OKV, 2048), cc_k_out[hp * 128:(hp + 1) * 128, :], w=w_kv)
                dio(pb(OKV + 2048, 2048), cc_k_out[1026 + hp * 128:1026 + (hp + 1) * 128, :])
                for b8 in range(4):
                    st_["kv_io"] = dio(dst3(OKV + 4096 + b8 * 1024, 8, 128),
                                       src3(ccvout[b8 * 1024:(b8 + 1) * 1024,
                                                   hp * 128:(hp + 1) * 128]))
                for qt in range(nqt):
                    nl = n_live(qt) if is_a else 8
                    npm = 1 if is_a else 0
                    n_st = nl + npm
                    lbase = PL0 + (qt % 2) * 32
                    if is_a:
                        for cpre in range(min(2, nl)):
                            msk_io[cpre % 2] = dio(
                                pb(MSK[cpre % 2], 512),
                                di["masksel"][(qt * 8 + cpre) * 128:(qt * 8 + cpre + 1) * 128, :],
                                w=[("PE", msk_pe[cpre % 2])])
                    c_exp = {(h_, p_): P.val("AC") for h_ in (0, 1) for p_ in (0, 1)}
                    c_ptev = {0: P.val("VE"), 1: P.val("VE")}

                    def emit_scores(ci, qt=qt, nl=nl, lbase=lbase, hp=hp,
                                    c_exp=c_exp):
                        is_pm = is_a and (ci == nl)
                        for hd in range(2):
                            q_lh = pb(OQ + hp * 2048 + qt * 128, 128, hd * 64, 64)
                            if is_pm:
                                c_pe = mm(PSM[:, hd * 4:hd * 4 + 4], q_lh,
                                          PMKt[hd * 64:(hd + 1) * 64, hp * 4:hp * 4 + 4],
                                          True, True,
                                          w=[("AC", max(c_exp[(0, 0)], c_exp[(0, 1)])),
                                             ("AC", max(c_exp[(1, 0)], c_exp[(1, 1)]))],
                                          i=[("PE", 1)])
                                c_ac = astep(lambda E, hd=hd, ci=ci, lbase=lbase: E.activation(
                                    pb(Pslot[hd * 2 + ci % 2], 4), PSM[:, hd * 4:hd * 4 + 4],
                                    AF.Exp, bias=BANKS[:, BNEGB:BNEGB + 1], scale=0.125,
                                    accum_out=pf(lbase + hd * 16 + ci, 1)),
                                    w=[("PE", c_pe)], i=[("AC", 1)])
                            else:
                                psS = PS[hd + 2 * (ci % 2)]
                                wS = [("AC", c_exp[(hd, ci % 2)]), ("IO", st_["kv_io"])]
                                mm(psS[:, :], q_lh, pb(OKV + ci * 512, 512, hd * 64, 64),
                                   True, False, w=wS)
                                if is_a:
                                    c_pe = mm(psS[:, :], IDB, pb(MSK[ci % 2], 512),
                                              False, True,
                                              w=[("IO", msk_io[ci % 2])], i=[("PE", 1)])
                                    if hd == 1:
                                        msk_pe[ci % 2] = c_pe
                                        if ci + 2 < nl:
                                            msk_io[ci % 2] = dio(
                                                pb(MSK[ci % 2], 512),
                                                di["masksel"][(qt * 8 + ci + 2) * 128:
                                                              (qt * 8 + ci + 3) * 128, :],
                                                w=[("PE", c_pe)])
                                else:
                                    tp = st_["t2ctr"] & 1
                                    st_["t2ctr"] += 1
                                    t2i, t2n = T2IN_B[tp], T2N_B[tp]
                                    c_ve = vstep(lambda E, ci=ci, qt=qt, hd=hd, t2i=t2i:
                                                 E.tensor_scalar(
                                        pb(t2i, 512), PFt[:, ci * 512:(ci + 1) * 512],
                                        PFt[:, PUW + qt * 32 + hd:PUW + qt * 32 + hd + 1],
                                        PFt[:, PUW + qt * 32 + 16 + hd:PUW + qt * 32 + 16 + hd + 1],
                                        op0=ALU.mult, op1=ALU.subtract),
                                        w=[("PO", st_["t2po2"][tp])], i=[("VE", 1)])
                                    c_po = gstep(lambda E, t2i=t2i, t2n=t2n: E.tensor_tensor(
                                        pb(t2n, 512), pb(t2i, 512), pb(t2i, 512),
                                        op=ALU.mult),
                                        w=[("VE", c_ve), ("PE", st_["t2pe2"][tp])], i=[("PO", 1)])
                                    st_["t2po2"][tp] = c_po
                                    c_pe = mm(psS[:, :], NIDB, pb(t2n, 512), False, True,
                                              w=[("PO", c_po)], i=[("PE", 1)])
                                    st_["t2pe2"][tp] = c_pe
                                c_ac = astep(lambda E, hd=hd, ci=ci, psS=psS, lbase=lbase:
                                             E.activation(
                                                 pb(Pslot[hd * 2 + ci % 2], 512), psS[:, :],
                                                 AF.Exp, bias=BANKS[:, BNEGB:BNEGB + 1],
                                                 scale=0.125,
                                                 accum_out=pf(lbase + hd * 16 + ci, 1)),
                                             w=[("PE", c_pe)], i=[("AC", 1)])
                            c_exp[(hd, ci % 2)] = c_ac

                    def emit_tail(ci, qt=qt, nl=nl, hp=hp, c_exp=c_exp, c_ptev=c_ptev):
                        is_pm = is_a and (ci == nl)
                        nblk = 1 if is_pm else 4
                        for hd in range(2):
                            c_ac = c_exp[(hd, ci % 2)]
                            wtr = [("AC", c_ac), ("VE", c_ptev[0]), ("VE", c_ptev[1])]
                            ps_ = Pslot[hd * 2 + ci % 2]
                            if is_pm:
                                c_pe = trg(PST[0:4, hd * 512:hd * 512 + 128],
                                           pb(ps_, 4), IDB, True, True, w=wtr, i=[("PE", 1)])
                                c_ve = vstep(lambda E, hd=hd: E.tensor_copy(
                                    pb(PTs[hd], 128, 0, 4), PST[0:4, hd * 512:hd * 512 + 128]),
                                    w=[("PE", c_pe)], i=[("VE", 1)])
                            else:
                                for k in range(4):
                                    c_pe = trg(PST[:, hd * 512 + k * 128: hd * 512 + (k + 1) * 128],
                                               pb(ps_ + k * 128, 128), IDB, k == 0, k == 3,
                                               w=wtr if k == 0 else (),
                                               i=[("PE", 1)] if k == 3 else ())
                                c_ve = vstep(lambda E, hd=hd: E.tensor_copy(
                                    pb(PTs[hd], 512), PST[:, hd * 512:(hd + 1) * 512]),
                                    w=[("PE", c_pe)], i=[("VE", 1)])
                            c_ptev[hd] = c_ve
                            psAV = PS[4 + hd]
                            for k in range(nblk):
                                if is_pm:
                                    c_avpe = mm(psAV[:, 0:64], pb(PTs[hd], 128, 0, 4),
                                                PMVt[0:4, hp * 128 + hd * 64:
                                                     hp * 128 + (hd + 1) * 64],
                                                False, True, w=[("VE", c_ve)], i=[("PE", 1)])
                                else:
                                    blk = ci * 4 + k
                                    wav = [("VE", c_ve)] if k == 0 else []
                                    if ci == 0 and k == 0:
                                        wav.append(("VE", st_["ofve"]))
                                    c_avpe = mm(psAV[:, 0:64], pb(PTs[hd] + k * 128, 128),
                                                pb(OKV + 4096 + blk * 128 + hd * 64, 64),
                                                ci == 0 and k == 0,
                                                (not is_a) and ci == 7 and k == 3,
                                                w=wav, i=[("PE", 1)])
                        st_["avpe"] = c_avpe

                    for stg in range(n_st + 1):
                        if stg < n_st:
                            emit_scores(stg)
                        if stg >= 1:
                            emit_tail(stg - 1)
                    nls = n_st
                    c0 = vstep(lambda E, nls=nls, lbase=lbase: E.tensor_reduce(
                        BANKS[:, BLS0:BLS0 + 1], pf(lbase, nls), axis=AX.X, op=ALU.add),
                        w=[("AC", max(c_exp[(0, 0)], c_exp[(0, 1)])),
                           ("AC", max(c_exp[(1, 0)], c_exp[(1, 1)]))], i=[("VE", 1)])
                    c1 = vstep(lambda E, nls=nls, lbase=lbase: E.tensor_reduce(
                        BANKS[:, BLS1:BLS1 + 1], pf(lbase + 16, nls), axis=AX.X, op=ALU.add),
                        i=[("VE", 1)])
                    c0 = vstep(lambda E: E.reciprocal(BANKS[:, BLI0:BLI0 + 1],
                                                      BANKS[:, BLS0:BLS0 + 1]),
                               w=[("VE", c1)], i=[("VE", 1)])
                    c1 = vstep(lambda E: E.reciprocal(BANKS[:, BLI1:BLI1 + 1],
                                                      BANKS[:, BLS1:BLS1 + 1]),
                               w=[("VE", c0)], i=[("VE", 1)])
                    vstep(lambda E: E.tensor_scalar(pb(OSB, 64), PS[4][:, 0:64],
                                                    BANKS[:, BLI0:BLI0 + 1], None, op0=ALU.mult),
                          w=[("VE", c1), ("PE", st_["avpe"]), ("VE", c_ptev[0]),
                             ("VE", c_ptev[1])])
                    c_ve = vstep(lambda E: E.tensor_scalar(pb(OSB + 64, 64), PS[5][:, 0:64],
                                                           BANKS[:, BLI1:BLI1 + 1], None,
                                                           op0=ALU.mult),
                                 i=[("VE", 1)])
                    c_pe = trg(PST[:, 0:128], pb(OSB, 128), IDB, True, True,
                               w=[("VE", c_ve)], i=[("PE", 1)])
                    st_["ofve"] = vstep(lambda E, hp=hp, qt=qt: E.tensor_copy(
                        pb(OH + hp * 2048 + qt * 128, 128), PST[:, 0:128]),
                        w=[("PE", c_pe)], i=[("VE", 1)])

        def phase_attn_a(lv=4):
            ln_phase("rms")
            barrier()
            if lv < 2:
                return
            qmax = [0]
            kmax = [0]
            proj_fm(di["wqkvT"], 1024, 8,
                    mk_post_rope(None, (cc_k_in, 0), PMXK, kmax))
            proj_v_nobias(di["wqkvT"], 2048)
            c_ve = finalize_max(PMXK, kmax[0], PMXK + 38)
            dwl(cc_k_in[1025:1026, 0:1], pf(PMXK + 38, 1, 0, 1), w=[("VE", c_ve)])
            c_cc = gstep(lambda E: E.collective_compute(
                "AllGather", ALU.bypass, replica_groups=CCG,
                ins=[cc_k_in.ap()], outs=[cc_k_out.ap()]),
                w=[("IO", P.val("IO")), ("WL", P.val("WL"))], i=[("CC", 1)])
            c_cc = gstep(lambda E: E.collective_compute(
                "AllGather", ALU.bypass, replica_groups=CCG,
                ins=[ccvin.ap()], outs=[ccvout.ap()]), i=[("CC", 1)])
            proj_fm(di["wqkvT"], 0, 8, mk_post_rope(OQ, None, PMXQ, qmax))
            c_ve = finalize_max(PMXQ, qmax[0], PMXQ + 38)
            c_sq = vstep(lambda E: E.tensor_tensor(pb(OWS, 32), PMKt[:, :], PMKt[:, :], op=ALU.mult),
                         i=[("VE", 1)])
            c_pe = mm(PSM[0:1, 32:64], ONES[:, 0:1], pb(OWS, 32), True, True,
                      w=[("VE", c_sq)], i=[("PE", 1)])
            c_ve2 = vstep(lambda E: E.tensor_reduce(pf(PMXK + 39, 1, 0, 1), PSM[0:1, 32:64],
                                                    axis=AX.X, op=ALU.max),
                          w=[("PE", c_pe)], i=[("VE", 1)])
            barrier()
            dwl(pf(PMXK + 36, 1, 0, 1), cc_k_out[1025:1026, 0:1], w=[("CC", P.val("CC"))])
            c_wl = dwl(pf(PMXK + 37, 1, 0, 1), cc_k_out[2051:2052, 0:1])
            make_negB(PMXQ + 38, [PMXK + 36, PMXK + 37, PMXK + 39], w=[("WL", c_wl)])
            barrier()
            if lv < 3:
                return
            attn_core(True)
            barrier()
            if lv < 4:
                return
            proj_fm(di["woutT"], 0, 8, post_accum_x(None))
            barrier()

        def phase_ffn():
            ln_phase("plain")
            barrier()
            WRS = OWS + 3952
            c_wl = dwl(dst3(WRS, 8, 7), src3(di["wrT"][:, :]))
            WSM = OWS + 3840
            for t16 in range(16):
                w0 = [("WL", c_wl)] if t16 == 0 else [("VE", P.val("VE"))]
                for kc in range(8):
                    mm(PSM[:, 0:7], pb(OH + kc * 2048 + t16 * 128, 128),
                       pb(WRS + kc * 7, 7), kc == 0, False, w=w0 if kc == 0 else ())
                c_pe = mm(PSM[:, 0:7], ONES[0:1, :], ROWSB[0:1, 1024:1031], False, True,
                          i=[("PE", 1)])
                EL = pf(PWK, 7)
                M1 = pf(PWK + 8, 1)
                EE = pf(PWK + 16, 7)
                EQ = pf(PWK + 24, 7)
                M2 = pf(PWK + 32, 1)
                GE = pf(PWK + 40, 7)
                DEN = pf(PWK + 48, 1)
                DIV = pf(PWK + 56, 1)
                WW = pf(PWK + 64, 7)
                c = vstep(lambda E: E.tensor_copy(EL, PSM[:, 0:7]), w=[("PE", c_pe)], i=[("VE", 1)])
                c = vstep(lambda E: E.tensor_reduce(M1, EL, axis=AX.X, op=ALU.max),
                          w=[("VE", c)], i=[("VE", 1)])
                c = vstep(lambda E: E.tensor_scalar(pf(PWK + 9, 1), M1, -1.0, None, op0=ALU.mult),
                          w=[("VE", c)], i=[("VE", 1)])
                c_ac = astep(lambda E: E.activation(EE, EL, AF.Exp, bias=pf(PWK + 9, 1), scale=1.0),
                             w=[("VE", c)], i=[("AC", 1)])
                c = vstep(lambda E: E.tensor_reduce(M1, EE, axis=AX.X, op=ALU.max),
                          w=[("AC", c_ac)], i=[("VE", 1)])
                c = vstep(lambda E: E.tensor_scalar(EQ, EE, M1, None, op0=ALU.is_equal),
                          w=[("VE", c)], i=[("VE", 1)])
                c = vstep(lambda E: E.scalar_tensor_tensor(EQ, EQ, M1, EE, op0=ALU.mult,
                                                           op1=ALU.subtract), w=[("VE", c)], i=[("VE", 1)])
                c = vstep(lambda E: E.tensor_reduce(M2, EQ, axis=AX.X, op=ALU.min),
                          w=[("VE", c)], i=[("VE", 1)])
                c = vstep(lambda E: E.tensor_scalar(M2, M2, -1.0, None, op0=ALU.mult),
                          w=[("VE", c)], i=[("VE", 1)])
                c = vstep(lambda E: E.tensor_scalar(GE, EE, M2, None, op0=ALU.is_ge),
                          w=[("VE", c)], i=[("VE", 1)])
                c = vstep(lambda E: E.tensor_scalar(DEN, M1, M2, None, op0=ALU.add),
                          w=[("VE", c)], i=[("VE", 1)])
                c = vstep(lambda E: E.reciprocal(DIV, DEN), w=[("VE", c)], i=[("VE", 1)])
                c = vstep(lambda E: E.scalar_tensor_tensor(WW, EE, DIV, GE, op0=ALU.mult,
                                                           op1=ALU.mult), w=[("VE", c)], i=[("VE", 1)])
                c = vstep(lambda E, t16=t16: E.tensor_copy(pb(WSM + t16 * 7, 7), WW),
                          w=[("VE", c)], i=[("VE", 1)])
            barrier()
            OIN = OQ
            WB = OWS + 2816
            WROW = OWS + 3328
            gs = [OKV, OKV + 2048]
            us = [OKV + 4096, OKV + 6144]
            ds = [OWA, OWA + 2816]
            c_gfree = [(), ()]
            c_ufree = [(), ()]
            c_dfree = [(), ()]
            c_silu = [0, 0]
            c_inn = [0, 0]
            c_dpe = [0, 0]
            c_gupe = [0, 0]
            for e in range(8):
                if e == 0:
                    gsrc, usrc, dsrc = di["sgT"][:, :], di["suT"][:, :], di["sdT"][:, :]
                else:
                    gsrc, usrc, dsrc = di["rgT"][e - 1], di["ruT"][e - 1], di["rdT"][e - 1]
                for T in range(4):
                    c_gwl = {}
                    c_uwl = {}
                    for j in range(2):
                        wslot = (c_gfree[j] if T == 0 and e == 0 else [("PE", c_gupe[j])])
                        c_gwl[j] = dwl(dst3(gs[j], 8, 256),
                                       src3(gsrc[:, j * 256:(j + 1) * 256]), w=wslot)
                        c_uwl[j] = dwl(dst3(us[j], 8, 256),
                                       src3(usrc[:, j * 256:(j + 1) * 256]))
                    c_dwl = {}
                    for o in range(2):
                        c_dwl[o] = dwl(dst3(ds[o], 22, 128),
                                       src3(dsrc[:, o * 128:(o + 1) * 128]),
                                       w=(c_dfree[o] if T == 0 and e == 0 else [("PE", c_dpe[o])]))
                    nexto = [2]
                    if e > 0:
                        c_pes = []
                        for jj in range(4):
                            t16 = T * 4 + jj
                            c_pes.append(trg(PST[0:1, jj * 128:(jj + 1) * 128],
                                             pb(WSM + t16 * 7 + (e - 1), 1), IDB,
                                             jj == 0, jj == 3,
                                             w=[("VE", P.val("VE"))] if jj == 0 else (),
                                             i=[("PE", 1)]))
                        c_ve = vstep(lambda E: E.tensor_copy(pb(WROW, 512, 0, 1), PST[0:1, 0:512]),
                                     w=[("PE", c_pes[-1])], i=[("VE", 1)])
                        c_pe = mm(PS[4][:, :], ONES[0:1, :], pb(WROW, 512, 0, 1), True, True,
                                  w=[("VE", c_ve), ("VE", P.val("VE"))], i=[("PE", 1)])
                        c_wbve = vstep(lambda E: E.tensor_copy(pb(WB, 512), PS[4][:, :]),
                                       w=[("PE", c_pe)], i=[("VE", 1)])
                    for ic in range(22):
                        j, which = divmod(ic, 2)
                        psG = PS[ic % 2]
                        psU = PS[2 + ic % 2]
                        wg = []
                        if which == 0:
                            wg.append(("WL", c_gwl[j]))
                        if c_silu[ic % 2]:
                            wg.append(("AC", c_silu[ic % 2]))
                        for kc in range(8):
                            mm(psG[:, :], pb(gs[j % 2] + kc * 256 + which * 128, 128),
                               pb(OH + kc * 2048 + T * 512, 512), kc == 0, kc == 7,
                               w=wg if kc == 0 else ())
                        wu = []
                        if which == 0:
                            wu.append(("WL", c_uwl[j]))
                        if c_inn[ic % 2]:
                            wu.append(("VE", c_inn[ic % 2]))
                        for kc in range(8):
                            c_pe = mm(psU[:, :], pb(us[j % 2] + kc * 256 + which * 128, 128),
                                      pb(OH + kc * 2048 + T * 512, 512), kc == 0, kc == 7,
                                      w=wu if kc == 0 else (),
                                      i=[("PE", 1)] if kc == 7 else ())
                        c_ac = astep(lambda E, ic=ic, psG=psG: E.activation(
                            pb(OIN + ic * 512, 512), psG[:, :], AF.Silu),
                            w=[("PE", c_pe)], i=[("AC", 1)])
                        c_silu[ic % 2] = c_ac
                        c_ve = vstep(lambda E, ic=ic, psU=psU: E.tensor_tensor(
                            pb(OIN + ic * 512, 512), pb(OIN + ic * 512, 512), psU[:, :],
                            op=ALU.mult), w=[("AC", c_ac)], i=[("VE", 1)])
                        c_inn[ic % 2] = c_ve
                        if which == 1:
                            c_gupe[j % 2] = c_pe
                            if j + 2 < 11:
                                jn = j + 2
                                c_gwl[jn] = dwl(dst3(gs[jn % 2], 8, 256),
                                                src3(gsrc[:, jn * 256:(jn + 1) * 256]),
                                                w=[("PE", c_pe)])
                                c_uwl[jn] = dwl(dst3(us[jn % 2], 8, 256),
                                                src3(usrc[:, jn * 256:(jn + 1) * 256]))
                    c_dve = [P.val("VE")] * 2
                    for o in range(8):
                        psD = PS[4 + o % 2]
                        wd = [("WL", c_dwl[o]), ("VE", c_inn[0]), ("VE", c_inn[1]),
                              ("VE", c_dve[o % 2])]
                        if e > 0:
                            wd.append(("VE", c_wbve))
                        for ic in range(22):
                            c_pe = mm(psD[:, :], pb(ds[o % 2] + ic * 128, 128),
                                      pb(OIN + ic * 512, 512), ic == 0, ic == 21,
                                      w=wd if ic == 0 else (),
                                      i=[("PE", 1)] if ic == 21 else ())
                        xsl = XR[:, o * 2048 + T * 512: o * 2048 + (T + 1) * 512]
                        if e == 0:
                            c_ve = vstep(lambda E, psD=psD, xsl=xsl: E.tensor_tensor(
                                xsl, xsl, psD[:, :], op=ALU.add), w=[("PE", c_pe)], i=[("VE", 1)])
                        else:
                            TMP = pf(PWK + 512, 512)
                            vstep(lambda E, psD=psD: E.tensor_tensor(
                                TMP, psD[:, :], pb(WB, 512), op=ALU.mult), w=[("PE", c_pe)])
                            c_ve = vstep(lambda E, xsl=xsl: E.tensor_tensor(
                                xsl, xsl, TMP, op=ALU.add), i=[("VE", 1)])
                        c_dve[o % 2] = c_ve
                        c_dpe[o % 2] = c_pe
                        if nexto[0] < 8:
                            on = nexto[0]
                            nexto[0] += 1
                            c_dwl[on] = dwl(dst3(ds[on % 2], 22, 128),
                                            src3(dsrc[:, on * 128:(on + 1) * 128]),
                                            w=[("PE", c_pe)])
            barrier()

        def phase_attn_b():
            ln_phase("plain")
            barrier()
            qmax = [0]
            kmax = [0]
            proj_fm(di["wq"], 0, 8, post_plain(OQ, BBQ))
            st_ms = [0, 0]
            cellc = [0]
            for o in range(8):
                for T in range(4):
                    SQ = pb(OWS + (T % 2) * 512, 512)
                    c_sq = vstep(lambda E, o=o, T=T, SQ=SQ: E.tensor_tensor(
                        SQ, pb(OQ + o * 2048 + T * 512, 512), pb(OQ + o * 2048 + T * 512, 512),
                        op=ALU.mult), i=[("VE", 1)])
                    psROW = PS[4 + T % 2]
                    c_pe3 = mm(psROW[0:1, :], ONES[:, 0:1], SQ, True, True,
                               w=[("VE", c_sq), ("VE", st_ms[T % 2])], i=[("PE", 1)])
                    cell = cellc[0]
                    cellc[0] += 1
                    c_red = vstep(lambda E, cell=cell, psROW=psROW: E.tensor_reduce(
                        pf(PMXQ + cell, 1, 0, 1), psROW[0:1, :], axis=AX.X, op=ALU.max),
                        w=[("PE", c_pe3)], i=[("VE", 1)])
                    st_ms[T % 2] = c_red
            finalize_max(PMXQ, cellc[0], PMXQ + 38)
            c_glo = 0
            for c in range(8):
                c_glo = vstep(lambda E, c=c: E.tensor_reduce(
                    pf(PGLO + c, 1), pb(OQ + c * 2048, 2048), axis=AX.X, op=ALU.add),
                    i=[("VE", 1)])
            c_pe = tr(PSM[0:8, 128:256], pf(PGLO, 8), IDF[:, :], w=[("VE", c_glo)], i=[("PE", 1)])
            c_ve = vstep(lambda E: E.tensor_copy(V8[0:8, :], PSM[0:8, 128:256]),
                         w=[("PE", c_pe)], i=[("VE", 1)])
            dwl(cc_k_in[1024:1025, 0:1024].rearrange("a (b c) -> (a b) c", b=8), V8[0:8, :],
                w=[("VE", c_ve)])
            proj_fm(di["wk"], 0, 8, post_stage_bias(BBK, (cc_k_in, 0), PMXK, kmax))
            proj_v_bias(di["wv"], 0)
            c_ve = finalize_max(PMXK, kmax[0], PMXK + 38)
            dwl(cc_k_in[1025:1026, 0:1], pf(PMXK + 38, 1, 0, 1), w=[("VE", c_ve)])
            gstep(lambda E: E.collective_compute(
                "AllGather", ALU.bypass, replica_groups=CCG,
                ins=[cc_k_in.ap()], outs=[cc_k_out.ap()]),
                w=[("IO", P.val("IO")), ("WL", P.val("WL"))], i=[("CC", 1)])
            gstep(lambda E: E.collective_compute(
                "AllGather", ALU.bypass, replica_groups=CCG,
                ins=[ccvin.ap()], outs=[ccvout.ap()]), i=[("CC", 1)])
            proj_fm(di["wfq"], 0, 8, post_plain(OH), rhs_off=OQ)
            barrier()
            dwl(pf(PGLA, 128, 0, 8), cc_k_out[1024:1025, 0:1024].rearrange("a (b c) -> (a b) c", b=8),
                w=[("CC", P.val("CC"))])
            c_wl = dwl(pf(PGLB, 128, 0, 8), cc_k_out[2050:2051, 0:1024].rearrange("a (b c) -> (a b) c", b=8))
            c = vstep(lambda E: E.tensor_tensor(pf(PGLA, 128, 0, 8), pf(PGLA, 128, 0, 8),
                                                pf(PGLB, 128, 0, 8), op=ALU.add),
                      w=[("WL", c_wl)], i=[("VE", 1)])
            c = vstep(lambda E: E.tensor_scalar(pf(PGLA, 128, 0, 8), pf(PGLA, 128, 0, 8),
                                                1.0 / 4096.0, None, op0=ALU.mult),
                      w=[("VE", c)], i=[("VE", 1)])
            c_pe = tr(PSM[:, 32:40], pf(PGLA, 128, 0, 8), IDF[0:8, 0:8], w=[("VE", c)], i=[("PE", 1)])
            c = vstep(lambda E: E.tensor_copy(pf(PGLC, 8), PSM[:, 32:40]),
                      w=[("PE", c_pe)], i=[("VE", 1)])
            c = vstep(lambda E: E.tensor_copy(pb(OWS + 3072, 8), pf(PGLC, 8)),
                      w=[("VE", c)], i=[("VE", 1)])
            dwl(pf(PMXK + 36, 1, 0, 1), cc_k_out[1025:1026, 0:1])
            c_wl = dwl(pf(PMXK + 37, 1, 0, 1), cc_k_out[2051:2052, 0:1])
            make_negB(PMXQ + 38, [PMXK + 36, PMXK + 37], w=[("WL", c_wl), ("VE", c)])
            slots = [OWA, OWA + 1024]
            slot_free = [(), ()]
            for o in range(8):
                c_wl = dwl(dst3(slots[o % 2], 8, 128),
                           src3(di["wfg"][:, o * 128:(o + 1) * 128]),
                           w=slot_free[o % 2])
                for kc in range(8):
                    c_pe = mm(PSM[:, 40:41], pb(slots[o % 2] + kc * 128, 128),
                              pb(OWS + 3072 + kc, 1), kc == 0, kc == 7,
                              w=[("WL", c_wl), ("VE", P.val("VE"))] if kc == 0 else (),
                              i=[("PE", 1)] if kc == 7 else ())
                c_ve = vstep(lambda E, o=o: E.tensor_scalar(
                    BANKS[:, BGB + o:BGB + o + 1], PSM[:, 40:41],
                    BANKS[:, BBFQ + o:BBFQ + o + 1], BANKS[:, BBFG + o:BBFG + o + 1],
                    op0=ALU.add, op1=ALU.add), w=[("PE", c_pe)], i=[("VE", 1)])
                slot_free[o % 2] = [("PE", c_pe)]
            barrier()
            CS = [OWS + 1024, OWS + 1536]
            for T in range(4):
                for o in range(8):
                    c_ac = astep(lambda E, o=o, T=T, s=CS[o % 2]: E.activation(
                        pb(s, 512), pb(OH + o * 2048 + T * 512, 512), AF.Tanh,
                        bias=BANKS[:, BGB + o:BGB + o + 1]),
                        w=[("PE", P.val("PE"))] if o >= 2 else (), i=[("AC", 1)])
                    c_pe = mm(PS[4][0:32, :], UBt[:, o * 32:(o + 1) * 32], pb(CS[o % 2], 512),
                              o == 0, o == 7, w=[("AC", c_ac)], i=[("PE", 1)])
                c_ac = astep(lambda E, T=T: E.activation(
                    pf(PPZ + T * 512, 512, 0, 32), PS[4][0:32, :], AF.Sigmoid),
                    w=[("PE", c_pe)], i=[("AC", 1)])
                vstep(lambda E, T=T: E.tensor_scalar(
                    pf(PPZ + T * 512, 512, 0, 32), pf(PPZ + T * 512, 512, 0, 32), 4096.0,
                    None, op0=ALU.mult), w=[("AC", c_ac)], i=[("VE", 1)])
            barrier()
            for qt in range(16):
                c_pe = tr(PSM[:, 64:96], pf(PPZ + qt * 128, 128, 0, 32), IDF[0:32, 0:32],
                          w=[("VE", P.val("VE"))], i=[("PE", 1)])
                c = vstep(lambda E: E.reciprocal(pf(PRC, 16), PSM[:, 80:96]),
                          w=[("PE", c_pe)], i=[("VE", 1)])
                c = vstep(lambda E, qt=qt: E.tensor_scalar(
                    pf(PUW + qt * 32, 16), pf(PRC, 16), 4.0 * SQ8, None, op0=ALU.mult),
                    w=[("VE", c)], i=[("VE", 1)])
                c = vstep(lambda E, qt=qt: E.tensor_tensor(
                    pf(PUW + qt * 32 + 16, 16), pf(PUW + qt * 32, 16), PSM[:, 64:80],
                    op=ALU.mult), w=[("VE", c)], i=[("VE", 1)])
            dio(PFt[:, 0:4096], di["Jrow"][:, :])
            barrier()
            attn_core(False)
            barrier()
            proj_fm(di["wo"], 0, 8, post_accum_x(BBO))
            barrier()

        phase_attn_a(lv=min(nphases, 4))
        if nphases >= 5:
            phase_ffn()
        if nphases >= 6:
            phase_attn_b()
        if nphases >= 7:
            phase_ffn()

        for c in range(8):
            dio(y[c * 128:(c + 1) * 128, :], XR[:, c * 2048:(c + 1) * 2048],
                w=[("VE", P.val("VE"))] if c == 0 else ())
        for (tname, pool, p0, np_, c0, ncols) in debug_taps:
            src = {"PB": PBt, "PF": PFt, "XR": XR, "CONSTB": CONSTB}[pool]
            if pool in ("PB", "CONSTB"):
                dwl(taps[tname][:, :], src[p0:p0 + np_, c0:c0 + ncols])
            else:
                dio(taps[tname][:, :], src[p0:p0 + np_, c0:c0 + ncols])
        P.step("sync", None, [("IO", P.val("IO")), ("WL", P.val("WL"))])

        with nc.Block() as blk:
            @blk.sync
            def _(E):
                for f in P.steps["sync"]:
                    f(E)

            @blk.scalar
            def _(E):
                for f in P.steps["scalar"]:
                    f(E)

            @blk.vector
            def _(E):
                for f in P.steps["vector"]:
                    f(E)

            @blk.tensor
            def _(E):
                for f in P.steps["tensor"]:
                    f(E)

            @blk.gpsimd
            def _(E):
                for f in P.steps["gpsimd"]:
                    f(E)
    nc._plan = P
    return nc, di, taps


def kernel(**inputs):
    cores = make_core_inputs(inputs)
    nc, di, taps = build_nc()
    res = run_bass_kernel_spmd(nc, cores, core_ids=list(range(8)))
    out = np.empty((4, 4096, 1024), np.float32)
    for cid in range(8):
        b, h = cid // 2, cid % 2
        out[b, h * 2048:(h + 1) * 2048, :] = np.asarray(res.results[cid]["y"]).T
    return out
